# version 11
# baseline (speedup 1.0000x reference)
import contextlib
import numpy as np
import ml_dtypes
import concourse.bass as bass
import concourse.mybir as mybir
from concourse.bass_utils import run_bass_kernel_spmd

F32 = mybir.dt.float32
BF16 = mybir.dt.bfloat16
AL = mybir.AluOpType
AF = mybir.ActivationFunctionType
AX = mybir.AxisListType.X

NCORES = 8
TOKENS = 4096
TT = 512
NT = TOKENS // TT
TB = TT // 128
D, KC, MLP, MO, H, HD = 1024, 8, 4096, 32, 16, 64
HALF_PI = float(np.float32(np.pi / 2))
TWO_PI_HI = float(np.float32(2 * np.pi))
TWO_PI_LO = float(np.float32(2 * np.pi - np.float64(np.float32(2 * np.pi))))
PI = float(np.pi)
PI_CLAMP = 3.1415925


SERIALIZE = False


class Prog:

    def __init__(self):
        self.ops = []
        self.writer = {}
        self.readers = {}

    def op(self, engine, fn, reads=(), writes=(), inc=1, drain=False):
        drain = drain or engine in ("vector", "scalar")
        if SERIALIZE == 1:
            reads = list(reads) + ["SERIAL"]
            writes = list(writes) + ["SERIAL"]
        elif SERIALIZE == 2 and engine == "sync":
            reads = list(reads) + ["SERIAL"]
            writes = list(writes) + ["SERIAL"]
        elif SERIALIZE == 3 and engine == "tensor":
            reads = list(reads) + ["SERIAL"]
            writes = list(writes) + ["SERIAL"]
        elif SERIALIZE == 4 and engine in ("vector", "scalar"):
            reads = list(reads) + ["SERIAL"]
            writes = list(writes) + ["SERIAL"]
        i = len(self.ops)
        deps = set()
        for r in reads:
            w = self.writer.get(r)
            if w is not None:
                deps.add(w)
        for r in writes:
            w = self.writer.get(r)
            if w is not None:
                deps.add(w)
            deps.update(self.readers.get(r, ()))
        self.ops.append({"e": engine, "fn": fn, "deps": deps, "inc": inc,
                         "drain": drain})
        for r in reads:
            self.readers.setdefault(r, []).append(i)
        for r in writes:
            self.writer[r] = i
            self.readers[r] = []
        return i

    def emit(self, nc, stack):
        engines = []
        for o in self.ops:
            if o["e"] not in engines:
                engines.append(o["e"])
        ref = set()
        for o in self.ops:
            for d in o["deps"]:
                if self.ops[d]["e"] != o["e"]:
                    ref.add(d)
        for i, o in enumerate(self.ops):
            if o["e"] == "gpsimd" and o["inc"] == 16:
                ref.add(i)
        NPOOL = 8
        pool_engines = {o["e"] for o in self.ops if o["inc"] == 16}
        sems = {}
        pools = {}
        for e in engines:
            sems[e] = stack.enter_context(nc.semaphore("sem_" + e))
            if e in pool_engines:
                pools[e] = [stack.enter_context(nc.semaphore(f"dmap_{e}{k}"))
                            for k in range(NPOOL)]
        cum = {e: 0 for e in engines}
        pcount = {e: [0] * NPOOL for e in pool_engines}
        pidx = {e: 0 for e in pool_engines}
        semkey = [None] * len(self.ops)
        for i, o in enumerate(self.ops):
            e = o["e"]
            if o["inc"] == 16:
                k = pidx[e]
                pidx[e] = (k + 1) % NPOOL
                pcount[e][k] += 16
                semkey[i] = (("pool", e, k), pcount[e][k])
            else:
                if i in ref:
                    cum[e] += o["inc"]
                semkey[i] = (("eng", e), cum[e])

        def sem_of(key):
            return pools[key[1]][key[2]] if key[0] == "pool" else sems[key[1]]

        final_gp = dict(pcount.get("gpsimd", {} and None) and {}) if False else None
        gp_final = (list(enumerate(pcount["gpsimd"]))
                    if "gpsimd" in pcount else [])

        def mk(e):
            def emit_engine(eng):
                wm = {}
                for i, o in enumerate(self.ops):
                    if o["e"] != e:
                        continue
                    for d in sorted(o["deps"]):
                        de = self.ops[d]["e"]
                        if de == e:
                            continue
                        key, v = semkey[d]
                        if wm.get(key, 0) < v:
                            eng.wait_ge(sem_of(key), v)
                            wm[key] = v
                    inst = o["fn"](eng)
                    if o["drain"]:
                        inst = eng.drain()
                    if o["inc"] == 16 or i in ref:
                        key, _ = semkey[i]
                        inst.then_inc(sem_of(key), o["inc"])
                if e == "gpsimd":
                    for k, v in gp_final:
                        if v:
                            eng.wait_ge(pools["gpsimd"][k], v)
            return emit_engine

        return engines, mk


def _bc(ap, dim, count):
    lst = [list(p) for p in ap.ap]
    assert lst[dim][1] == 1
    lst[dim] = [0, count]
    return bass.AP(ap.tensor, ap.offset, lst)


def _swap_free(ap3):
    lst = [list(p) for p in ap3.ap]
    return bass.AP(ap3.tensor, ap3.offset, [lst[0], lst[2], lst[1]])


def build_nc(n_tiles=NT, upto=99, taps=()):
    nc = bass.Bass("TRN2", target_bir_lowering=False)

    d_x = nc.dram_tensor("x", [TOKENS, D], F32, kind="ExternalInput").ap()
    d_w = {}
    for nm in ("wq", "wk", "wv", "wo"):
        d_w[nm] = nc.dram_tensor(nm, [D, D], F32, kind="ExternalInput").ap()
    for nm in ("fcw", "projw", "w1w", "w2w"):
        d_w[nm] = nc.dram_tensor(nm, [MLP, D], F32, kind="ExternalInput").ap()
    d_g1 = nc.dram_tensor("g1", [128, KC], F32, kind="ExternalInput").ap()
    d_be1 = nc.dram_tensor("be1", [128, KC], F32, kind="ExternalInput").ap()
    d_g2 = nc.dram_tensor("g2", [128, KC], F32, kind="ExternalInput").ap()
    d_be2 = nc.dram_tensor("be2", [128, KC], F32, kind="ExternalInput").ap()
    d_fcb = nc.dram_tensor("fcb", [128, MO], F32, kind="ExternalInput").ap()
    d_b1f = nc.dram_tensor("b1f", [128, MO], F32, kind="ExternalInput").ap()
    d_th = nc.dram_tensor("th", [128, MO], F32, kind="ExternalInput").ap()
    d_wrr = nc.dram_tensor("wr", [128, MO], F32, kind="ExternalInput").ap()
    d_wii = nc.dram_tensor("wi", [128, MO], F32, kind="ExternalInput").ap()
    d_brow = {nm: nc.dram_tensor(nm, [1, D], F32, kind="ExternalInput").ap()
              for nm in ("bq", "bk", "bv", "bo", "bpr", "bw2")}
    d_eye = nc.dram_tensor("eye", [128, 128], BF16, kind="ExternalInput").ap()
    d_y = nc.dram_tensor("y", [TOKENS, D], F32, kind="ExternalOutput").ap()
    d_rows = nc.dram_tensor("rows_scr", [12, D], BF16, kind="Internal").ap()
    d_wscr = {}
    for nm in ("wq", "wk", "wv", "wo"):
        d_wscr[nm] = nc.dram_tensor("scr_" + nm, [D, 2 * D], BF16, kind="Internal").ap()
    for nm in ("fcw", "projw", "w1w", "w2w"):
        d_wscr[nm] = nc.dram_tensor("scr_" + nm, [MLP, 2 * D], BF16, kind="Internal").ap()

    sb = lambda nm, sh, dt=F32: nc.alloc_sbuf_tensor(nm, sh, dt).ap()
    x_t = [sb(f"x{t}", [128, D]) for t in range(TB)]
    x0_t = [sb(f"xb{t}", [128, D]) for t in range(TB)]
    xhf = [sb(f"xhf{i}", [128, KC * TT], BF16) for i in range(2)]
    a_hi = [sb(f"ahi{t}", [128, D], BF16) for t in range(TB)]
    a_lo = [sb(f"alo{t}", [128, D], BF16) for t in range(TB)]
    xh_tm2 = [sb(f"xh_tm{i}", [128, D], BF16) for i in range(2)]
    xl_tm2 = [sb(f"xl_tm{i}", [128, D], BF16) for i in range(2)]
    jnk_tm = sb("jnk_tm", [128, D], BF16)
    fq = sb("fq", [128, TB * D])
    fkb = sb("fkb", [128, D])
    fvb = sb("fvb", [128, D])
    qkvb = sb("qkvb", [128, D])
    Ab = sb("Ab", [128, TB * H * H])
    qks = sb("qks", [128, H])
    rcp = sb("rcp", [128, H])
    scr = sb("scr", [128, D])
    g32 = sb("g32", [128, TT])
    hbuf = sb("hbuf", [128, TT])
    h_hi = sb("h_hi", [128, MO * TT], BF16)
    h_lo = sb("h_lo", [128, MO * TT], BF16)
    wbuf = [sb("wbuf0", [128, D])]
    wbh = [sb(f"wbh{i}", [128, D], BF16) for i in range(2)]
    wbl = [sb(f"wbl{i}", [128, D], BF16) for i in range(2)]
    NWP = 3
    wpair = [sb(f"wpair{i}", [128, 2 * D], BF16) for i in range(NWP)]
    stat = sb("stat", [128, 3 * TB])
    eye_s = sb("eye_s", [128, 128], BF16)
    ones_r = sb("ones_r", [2, 128], BF16)
    g1_s = sb("g1_s", [128, KC]); be1_s = sb("be1_s", [128, KC])
    g2_s = sb("g2_s", [128, KC]); be2_s = sb("be2_s", [128, KC])
    fcb_s = sb("fcb_s", [128, MO]); b1f_s = sb("b1f_s", [128, MO])
    th_s = sb("th_s", [128, MO]); wr_s = sb("wr_s", [128, MO]); wi_s = sb("wi_s", [128, MO])
    biasfc = sb("biasfc", [128, MO])
    b1sin = sb("b1sin", [128, MO]); b1cos = sb("b1cos", [128, MO])
    P1 = sb("P1", [128, MO]); P2 = sb("P2", [128, MO])
    sinb = sb("sinb", [128, MO]); cosb = sb("cosb", [128, MO])
    qtmp = sb("qtmp", [128, MO]); qmask = sb("qmask", [128, MO]); qpb = sb("qpb", [128, MO])
    brow32 = scr[0:1, :]
    brows2 = sb("brows2", [2, D], BF16)
    browh = brows2[0:1, :]
    browl = brows2[1:2, :]

    pA = nc.alloc_psum_tensor("pA", [128, D], F32).ap()
    pB = nc.alloc_psum_tensor("pB", [128, D], F32).ap()
    pC = nc.alloc_psum_tensor("pC", [128, TT], F32).ap()
    pD = nc.alloc_psum_tensor("pD", [128, TT], F32).ap()
    pE = nc.alloc_psum_tensor("pE", [128, TT], F32).ap()
    pF = nc.alloc_psum_tensor("pF", [128, TT], F32).ap()
    pEb = pE.bitcast(BF16)
    pFb = pF.bitcast(BF16)
    G = [pA[:, 0:TT], pA[:, TT:D], pB[:, 0:TT], pB[:, TT:D], pC, pD, pE, pF]
    GR = ["pA", "pA", "pB", "pB", "pC", "pD", "pE", "pF"]
    ALLG = ["pA", "pB", "pC", "pD", "pE", "pF"]

    g = Prog()

    for dst, src in ((g1_s, d_g1), (be1_s, d_be1), (g2_s, d_g2), (be2_s, d_be2),
                     (fcb_s, d_fcb), (b1f_s, d_b1f), (th_s, d_th), (wr_s, d_wrr),
                     (wi_s, d_wii), (eye_s, d_eye)):
        g.op("gpsimd", lambda e, a=dst, b=src: e.dma_start(out=a, in_=b),
             writes=[dst.tensor.name], inc=16)
    g.op("vector", lambda e: e.memset(ones_r, 1.0), writes=["ones_r"], drain=True)

    def wrap_ladder(e, shift):
        e.tensor_scalar(qtmp, th_s, shift, 0.0, AL.add, AL.add)
        e.drain()
        for _ in range(8):
            e.tensor_scalar(qmask, qtmp, PI, 0.0, AL.is_gt, AL.add)
            e.drain()
            e.scalar_tensor_tensor(qtmp, qmask, -TWO_PI_HI, qtmp, AL.mult, AL.add)
            e.drain()
            e.scalar_tensor_tensor(qtmp, qmask, -TWO_PI_LO, qtmp, AL.mult, AL.add)
            e.drain()
            e.tensor_scalar(qmask, qtmp, -PI, 0.0, AL.is_lt, AL.add)
            e.drain()
            e.scalar_tensor_tensor(qtmp, qmask, TWO_PI_HI, qtmp, AL.mult, AL.add)
            e.drain()
            e.scalar_tensor_tensor(qtmp, qmask, TWO_PI_LO, qtmp, AL.mult, AL.add)
            e.drain()
        return e.tensor_scalar(qtmp, qtmp, PI_CLAMP, -PI_CLAMP, AL.min, AL.max)

    g.op("vector", lambda e: wrap_ladder(e, 0.0),
         reads=["th_s"], writes=["qtmp", "qmask"], drain=True)
    g.op("scalar", lambda e: e.activation(sinb, qtmp, AF.Sin),
         reads=["qtmp"], writes=["sinb"], drain=True)
    g.op("vector", lambda e: wrap_ladder(e, HALF_PI),
         reads=["th_s", "sinb"], writes=["qtmp", "qmask"], drain=True)
    g.op("scalar", lambda e: e.activation(cosb, qtmp, AF.Sin),
         reads=["qtmp"], writes=["cosb"], drain=True)

    def qfin(e):
        e.tensor_tensor(qpb, cosb, wr_s, AL.mult)
        e.tensor_tensor(qtmp, sinb, wi_s, AL.mult)
        e.drain()
        e.tensor_tensor(qpb, qpb, qtmp, AL.add)
        e.drain()
        e.tensor_tensor(qpb, qpb, wi_s, AL.mult)
        e.drain()
        e.tensor_scalar(qpb, qpb, 0.1, 0.0, AL.mult, AL.add)
        e.drain()
        e.tensor_tensor(P1, qpb, sinb, AL.mult)
        e.tensor_tensor(P2, qpb, cosb, AL.mult)
        e.tensor_scalar(b1sin, b1f_s, 0.1, 0.0, AL.mult, AL.add)
        return e.tensor_scalar(b1cos, b1f_s, 0.1, HALF_PI, AL.mult, AL.add)
    g.op("vector", qfin,
         reads=["sinb", "cosb", "wr_s", "wi_s", "b1f_s"],
         writes=["P1", "P2", "qpb", "qtmp", "b1sin", "b1cos"], drain=True)

    def prep_weight_slow(wname):
        nrows = d_w[wname].shape[0]
        for c in range(nrows // 128):
            g.op("sync", lambda e, c=c, wname=wname: e.dma_start(
                out=wbuf[0], in_=d_w[wname][c * 128:(c + 1) * 128, :]),
                writes=["wbuf0"], inc=16)
            def cast_plain(e):
                e.tensor_copy(wbh[0], wbuf[0])
                e.drain()
                return e.tensor_tensor(wbl[0], wbuf[0], wbh[0], AL.subtract)
            g.op("vector", cast_plain, reads=["wbuf0"], writes=["wbh0", "wbl0"])
            g.op("sync", lambda e, c=c, wname=wname: e.dma_start(
                out=d_wscr[wname][c * 128:(c + 1) * 128, 0:D], in_=wbh[0]),
                reads=["wbh0"], writes=[f"wscr_{wname}"], inc=16)
            g.op("sync", lambda e, c=c, wname=wname: e.dma_start(
                out=d_wscr[wname][c * 128:(c + 1) * 128, D:2 * D], in_=wbl[0]),
                reads=["wbl0"], writes=[f"wscr_{wname}"], inc=16)

    def prep_weight(wname, gamma=None, beta=None, bias_col_psum=False):
        nrows = d_w[wname].shape[0]
        nch = nrows // 128
        for c in range(nch):
            j = c % 2
            win = wpair[j].bitcast(F32)
            whi = wpair[2][:, 0:D]
            wlo = wpair[2][:, D:2 * D]
            g.op("sync", lambda e, c=c, win=win, wname=wname: e.dma_start(
                out=win, in_=d_w[wname][c * 128:(c + 1) * 128, :]),
                writes=[f"wpair{j}"], inc=16)
            if gamma is None:
                def cast_plain(e, win=win, whi=whi, wlo=wlo):
                    e.tensor_copy(whi, win)
                    e.drain()
                    return e.tensor_tensor(wlo, win, whi, AL.subtract)
                g.op("vector", cast_plain, reads=[f"wpair{j}"],
                     writes=["wpair2"])
            elif gamma == "bcast":
                def cast_fold2(e, win=win, whi=whi, wlo=wlo):
                    gb = _bc(g2_s.rearrange("p (a b) -> p a b", b=1), 2, 128)
                    w3 = win.rearrange("p (a b) -> p a b", a=KC)
                    e.tensor_tensor(scr.rearrange("p (a b) -> p a b", a=KC),
                                    w3, gb, AL.mult)
                    e.drain()
                    e.tensor_copy(whi, scr)
                    e.drain()
                    return e.tensor_tensor(wlo, scr, whi, AL.subtract)
                g.op("vector", cast_fold2,
                     reads=[f"wpair{j}", "g2_s"], writes=["wpair2", "scr"])
            else:
                def cast_gam(e, c=c, win=win, whi=whi, wlo=wlo, gamma=gamma):
                    e.tensor_scalar(whi, win, gamma[:, c % KC:c % KC + 1], 0.0,
                                    AL.mult, AL.add)
                    e.drain()
                    return e.scalar_tensor_tensor(
                        wlo, win, gamma[:, c % KC:c % KC + 1], whi,
                        AL.mult, AL.subtract)
                g.op("vector", cast_gam,
                     reads=[f"wpair{j}", gamma.tensor.name], writes=["wpair2"])
            g.op("sync", lambda e, c=c, wname=wname: e.dma_start(
                out=d_wscr[wname][c * 128:(c + 1) * 128, :], in_=wpair[2]),
                reads=["wpair2"], writes=[f"wscr_{wname}"], inc=16)
            if beta is not None and not bias_col_psum:
                g.op("tensor", lambda e, c=c, win=win: (
                    e.matmul(pC[0:1, :], beta[:, c:c + 1], win[:, 0:TT],
                             start=(c == 0), stop=(c == KC - 1)),
                    e.matmul(pD[0:1, :], beta[:, c:c + 1], win[:, TT:D],
                             start=(c == 0), stop=(c == KC - 1)))[-1],
                    reads=[f"wpair{j}", beta.tensor.name], writes=["pC", "pD"])
            if beta is not None and bias_col_psum:
                g.op("tensor", lambda e, c=c, win=win: [
                    e.matmul(pC[:, c:c + 1], win[:, kc * 128:(kc + 1) * 128],
                             beta[:, kc:kc + 1], start=(kc == 0),
                             stop=(kc == KC - 1))
                    for kc in range(KC)][-1],
                    reads=[f"wpair{j}", beta.tensor.name], writes=["pC"])

    ROWIDX = {"bq": 0, "bk": 1, "bv": 2, "bo": 3, "bpr": 4, "bw2": 5}

    def prep_bias_row(nm, with_corr=False):
        r = ROWIDX[nm]
        g.op("gpsimd", lambda e: e.dma_start(out=brow32, in_=d_brow[nm]),
             writes=["brow32", "scr"], inc=16)
        if with_corr:
            g.op("vector", lambda e: (
                e.tensor_tensor(brow32[:, 0:TT], pC[0:1, :], brow32[:, 0:TT], AL.add),
                e.tensor_tensor(brow32[:, TT:D], pD[0:1, :], brow32[:, TT:D], AL.add))[-1],
                reads=["pC", "pD"], writes=["brow32", "scr", "pC", "pD"], drain=True)
        hi_st = qkvb[0:1, 0:TT].bitcast(BF16)
        lo_st = qkvb[0:1, TT:D].bitcast(BF16)
        def row_fin(e):
            e.tensor_copy(hi_st, brow32)
            e.drain()
            return e.tensor_tensor(lo_st, brow32, hi_st, AL.subtract)
        g.op("vector", row_fin, reads=["brow32", "scr"],
             writes=["rowstage", "qkvb"], drain=True)
        g.op("gpsimd", lambda e, r=r: e.dma_start(out=d_rows[r:r + 1, :], in_=hi_st),
             reads=["rowstage"], writes=[f"rows{r}"], inc=16)
        g.op("gpsimd", lambda e, r=r: e.dma_start(out=d_rows[6 + r:7 + r, :], in_=lo_st),
             reads=["rowstage"], writes=[f"rows{6 + r}"], inc=16)

    prep_weight("wq", gamma=g1_s, beta=be1_s)
    prep_bias_row("bq", with_corr=True)
    prep_weight("wk", gamma=g1_s, beta=be1_s)
    prep_bias_row("bk", with_corr=True)
    prep_weight("wv", gamma=g1_s, beta=be1_s)
    prep_bias_row("bv", with_corr=True)
    prep_weight("fcw", gamma="bcast", beta=be2_s, bias_col_psum=True)
    g.op("vector", lambda e: e.tensor_tensor(biasfc, pC[:, 0:MO], fcb_s, AL.add),
         reads=["pC", "fcb_s"], writes=["biasfc", "pC"], drain=True)
    prep_bias_row("bo")
    prep_bias_row("bpr")
    prep_bias_row("bw2")
    prep_weight("wo")
    prep_weight("projw")
    prep_weight("w1w")
    prep_weight("w2w")

    def load_bias_rows(nm):
        r = ROWIDX[nm]
        g.op("sync", lambda e, r=r: e.dma_start(out=browh, in_=d_rows[r:r + 1, :]),
             reads=[f"rows{r}"], writes=["browh"], inc=16)
        g.op("sync", lambda e, r=r: e.dma_start(out=browl, in_=d_rows[6 + r:7 + r, :]),
             reads=[f"rows{6 + r}"], writes=["browl"], inc=16)

    def tok_proj_phase(wname, lhs_hi_fn, lhs_lo_fn, lhs_reads, tbs=(0, 1, 2, 3)):
        gis = [tb * 2 + nb for tb in tbs for nb in range(2)]
        wr = sorted({GR[gi] for gi in gis})
        for kc in range(KC):
            j = (kc + (0 if tbs[0] == 0 else KC)) % NWP if False else kc % NWP
            g.op("sync", lambda e, kc=kc, j=j: e.dma_start(
                out=wpair[j], in_=d_wscr[wname][kc * 128:(kc + 1) * 128, :]),
                reads=[f"wscr_{wname}"], writes=[f"wpair{j}"], inc=16)

            def mm(e, kc=kc, j=j):
                last = None
                if kc == 0:
                    for gi in gis:
                        nb = gi % 2
                        e.matmul(G[gi], ones_r, brows2[:, nb * TT:(nb + 1) * TT],
                                 start=True, stop=False)
                for gi in gis:
                    tb, nb = gi // 2, gi % 2
                    lh = lhs_hi_fn(kc, tb)
                    ll = lhs_lo_fn(kc, tb)
                    rh = wpair[j][:, nb * TT:(nb + 1) * TT]
                    rl = wpair[j][:, D + nb * TT:D + (nb + 1) * TT]
                    e.matmul(G[gi], lh, rh, start=False, stop=False)
                    e.matmul(G[gi], lh, rl, start=False, stop=False)
                    last = e.matmul(G[gi], ll, rh, start=False, stop=(kc == KC - 1))
                return last
            rds = [f"wpair{j}", "ones_r"] + lhs_reads
            if kc == 0:
                rds += ["browh", "browl"]
            g.op("tensor", mm, reads=rds, writes=wr)

    def xhf_hi(kc, tb):
        return xhf[0][:, kc * TT + tb * 128: kc * TT + (tb + 1) * 128]

    def xhf_lo(kc, tb):
        return xhf[1][:, kc * TT + tb * 128: kc * TT + (tb + 1) * 128]

    def elu_evac(tb, dst, dst_res):
        g0, g1 = G[tb * 2], G[tb * 2 + 1]
        r0, r1 = GR[tb * 2], GR[tb * 2 + 1]
        g.op("vector", lambda e: (
            e.tensor_scalar(scr[:, 0:TT], g0, 0.0, None, AL.min),
            e.tensor_scalar(scr[:, TT:D], g1, 0.0, None, AL.min),
            e.tensor_scalar(dst[:, 0:TT], g0, 0.0, None, AL.max),
            e.tensor_scalar(dst[:, TT:D], g1, 0.0, None, AL.max))[-1],
            reads=[r0, r1], writes=["scr", dst_res, r0, r1])
        g.op("scalar", lambda e: e.activation(scr, scr, AF.Exp),
             reads=["scr"], writes=["scr"])
        g.op("vector", lambda e: e.tensor_tensor(dst, dst, scr, AL.add),
             reads=["scr"], writes=[dst_res])

    def copy_evac(tb, dst, dst_res):
        g0, g1 = G[tb * 2], G[tb * 2 + 1]
        r0, r1 = GR[tb * 2], GR[tb * 2 + 1]
        g.op("vector", lambda e: (
            e.tensor_copy(dst[:, 0:TT], g0),
            e.tensor_copy(dst[:, TT:D], g1))[-1],
            reads=[r0, r1], writes=[dst_res, r0, r1])

    def resid_evac(tb):
        g0, g1 = G[tb * 2], G[tb * 2 + 1]
        r0, r1 = GR[tb * 2], GR[tb * 2 + 1]
        g.op("vector", lambda e, tb=tb: (
            e.tensor_tensor(x_t[tb][:, 0:TT], g0, x_t[tb][:, 0:TT], AL.add),
            e.tensor_tensor(x_t[tb][:, TT:D], g1, x_t[tb][:, TT:D], AL.add))[-1],
            reads=[r0, r1], writes=[f"x{tb}", r0, r1])

    def stage_A(tb):
        fq_t = fq[:, tb * D:(tb + 1) * D]
        def fn(e, tb=tb):
            fk3 = fkb.rearrange("p (g n) -> p g n", g=H)
            pr3 = scr.rearrange("p (g n) -> p g n", g=H)
            last = None
            for h in range(H):
                fqh = fq_t[:, h * HD:(h + 1) * HD].rearrange("p (g n) -> p g n", g=1)
                e.tensor_tensor(pr3, fk3, _bc(fqh, 1, H), AL.mult)
                last = e.tensor_reduce(
                    Ab[:, tb * H * H + h * H: tb * H * H + (h + 1) * H],
                    pr3, AX, AL.add, opt_output=False)
            return last
        g.op("vector", fn, reads=[f"fq{tb}", "fkb"], writes=[f"A{tb}", "scr"],
             drain=True)

    def stage_2(tb):
        A_t = Ab[:, tb * H * H:(tb + 1) * H * H]
        def fn(e, tb=tb, A_t=A_t):
            fv3 = fvb.rearrange("p (g n) -> p g n", g=H)
            pr3 = scr.rearrange("p (g n) -> p g n", g=H)
            e.tensor_reduce(qks, A_t.rearrange("p (g n) -> p g n", g=H), AX,
                            AL.add, opt_output=False)
            e.drain()
            e.tensor_scalar(rcp, qks, 1e-6, 0.0, AL.add, AL.add)
            e.drain()
            e.reciprocal(rcp, rcp)
            e.drain()
            for h in range(H):
                Ah = A_t[:, h * H:(h + 1) * H].rearrange("p (g n) -> p g n", n=1)
                e.tensor_tensor(pr3, fv3, _bc(Ah, 2, HD), AL.mult)
                e.tensor_reduce(qkvb[:, h * HD:(h + 1) * HD],
                                _swap_free(pr3), AX, AL.add, opt_output=False)
            e.drain()
            for h in range(H):
                sl = slice(h * HD, (h + 1) * HD)
                e.tensor_scalar(a_hi[tb][:, sl], qkvb[:, sl], rcp[:, h:h + 1],
                                0.0, AL.mult, AL.add)
            last = None
            for h in range(H):
                sl = slice(h * HD, (h + 1) * HD)
                last = e.scalar_tensor_tensor(a_lo[tb][:, sl], qkvb[:, sl],
                                              rcp[:, h:h + 1], a_hi[tb][:, sl],
                                              AL.mult, AL.subtract)
            return last
        g.op("vector", fn, reads=[f"A{tb}", "fvb"],
             writes=[f"a{tb}", "scr", "qkvb", "qks", "rcp"], drain=True)

    pAb = pA.bitcast(BF16)
    pBb = pB.bitcast(BF16)
    pCb = pC.bitcast(BF16)
    pDb = pD.bitcast(BF16)
    TRP = [((pEb, "pE"), (pFb, "pF")),
           ((pCb, "pC"), (pDb, "pD")),
           ((pAb[:, 0:D], "pA"), (pAb[:, D:2 * D], "pA")),
           ((pBb[:, 0:D], "pB"), (pBb[:, D:2 * D], "pB"))]

    def transpose_pair(src_hi, src_lo, reads, dst_view_fn, wr_res, slot=0):
        (ph, phr), (pl, plr) = TRP[slot]
        g.op("tensor", lambda e: ([
            e.transpose(ph[:, b * 128:(b + 1) * 128],
                        src_hi[:, b * 128:(b + 1) * 128], eye_s)
            for b in range(KC)] + [
            e.transpose(pl[:, b * 128:(b + 1) * 128],
                        src_lo[:, b * 128:(b + 1) * 128], eye_s)
            for b in range(KC)])[-1],
            reads=list(reads) + ["eye_s"], writes=[phr, plr])
        def evac(e):
            d0, d1 = dst_view_fn()
            e.tensor_copy(d0, ph.rearrange("p (a b) -> p a b", a=KC))
            return e.tensor_copy(d1, pl.rearrange("p (a b) -> p a b", a=KC))
        g.op("vector", evac, reads=[phr, plr], writes=list(wr_res) + [phr, plr])

    def ln_stage(tile, final=False, replace=False, slots=(2, 3, 0, 1)):
        for tb in range(TB):
            cn, cs = 3 * tb, 3 * tb + 2
            def stats_v(e, tb=tb, cn=cn, cs=cs):
                e.reduce_sum(stat[:, cs:cs + 1], x_t[tb], AX, opt_output=False)
                e.drain()
                return e.tensor_scalar(stat[:, cn:cn + 1], stat[:, cs:cs + 1],
                                       -1.0 / D, 0.0, AL.mult, AL.add)
            g.op("vector", stats_v, reads=[f"x{tb}"], writes=[f"st{tb}"],
                 drain=True)
        for tb in range(TB):
            cn, cs = 3 * tb, 3 * tb + 2
            def stats_s(e, tb=tb, cn=cn, cs=cs):
                return e.activation(jnk_tm, x_t[tb], AF.Square,
                                    bias=stat[:, cn:cn + 1],
                                    accum_out=stat[:, cs:cs + 1])
            g.op("scalar", stats_s, reads=[f"x{tb}", f"st{tb}"],
                 writes=[f"st{tb}", "jnk"], drain=True)
        for tb in range(TB):
            cr, cs = 3 * tb + 1, 3 * tb + 2
            g.op("vector", lambda e, cr=cr, cs=cs: e.tensor_scalar(
                stat[:, cr:cr + 1], stat[:, cs:cs + 1], 1.0 / D, 1e-5,
                AL.mult, AL.add),
                reads=[f"st{tb}"], writes=[f"st{tb}"], drain=True)
        for tb in range(TB):
            cr, cs = 3 * tb + 1, 3 * tb + 2
            def rstd_s(e, cr=cr, cs=cs):
                e.activation(stat[:, cs:cs + 1], stat[:, cr:cr + 1], AF.Ln)
                e.drain()
                return e.activation(stat[:, cr:cr + 1], stat[:, cs:cs + 1],
                                    AF.Exp, scale=-0.5)
            g.op("scalar", rstd_s, reads=[f"st{tb}"], writes=[f"st{tb}"],
                 drain=True)
        if final:
            for tb in range(TB):
                cn, cr = 3 * tb, 3 * tb + 1
                g.op("vector", lambda e, tb=tb, cn=cn, cr=cr: e.tensor_scalar(
                    x_t[tb], x_t[tb], stat[:, cn:cn + 1], stat[:, cr:cr + 1],
                    AL.add, AL.mult),
                    reads=[f"st{tb}"], writes=[f"x{tb}"])
                r0 = tile * TT + tb * 128
                g.op("gpsimd", lambda e, tb=tb, r0=r0: e.dma_start(
                    out=d_y[r0:r0 + 128, :], in_=x_t[tb]),
                    reads=[f"x{tb}"], inc=16)
            return
        for tb in range(TB):
            cn, cr = 3 * tb, 3 * tb + 1
            def apply_v(e, tb=tb, cn=cn, cr=cr, replace=replace):
                e.tensor_scalar(scr, x_t[tb], stat[:, cn:cn + 1],
                                stat[:, cr:cr + 1], AL.add, AL.mult)
                e.tensor_copy(xh_tm2[tb % 2], scr)
                ret = e.tensor_tensor(xl_tm2[tb % 2], scr, xh_tm2[tb % 2],
                                      AL.subtract)
                if replace:
                    ret = e.tensor_copy(x_t[tb], scr)
                return ret
            g.op("vector", apply_v, reads=[f"st{tb}"],
                 writes=[f"xt2_{tb % 2}", "scr", f"x{tb}"])
            def dv(tb=tb):
                v0 = xhf[0].rearrange("p (a b) -> p a b", a=KC)[:, :, tb * 128:(tb + 1) * 128]
                v1 = xhf[1].rearrange("p (a b) -> p a b", a=KC)[:, :, tb * 128:(tb + 1) * 128]
                return v0, v1
            transpose_pair(xh_tm2[tb % 2], xl_tm2[tb % 2], [f"xt2_{tb % 2}"],
                           dv, [f"xhf{tb}"], slot=slots[tb])

    def mlp_phase(wname, gamma, epi):
        for mo in range(MO):
            j = mo % NWP
            g.op("sync", lambda e, mo=mo, j=j: e.dma_start(
                out=wpair[j], in_=d_wscr[wname][mo * 128:(mo + 1) * 128, :]),
                reads=[f"wscr_{wname}"], writes=[f"wpair{j}"], inc=16)
            pf, pf_res = (pC, "pC") if mo % 2 == 0 else (pD, "pD")

            def mm(e, j=j, pf=pf):
                last = None
                for kc in range(KC):
                    lh = wpair[j][:, kc * 128:(kc + 1) * 128]
                    ll = wpair[j][:, D + kc * 128:D + (kc + 1) * 128]
                    rh = xhf[0][:, kc * TT:(kc + 1) * TT]
                    rl = xhf[1][:, kc * TT:(kc + 1) * TT]
                    e.matmul(pf, lh, rh, start=(kc == 0), stop=False)
                    e.matmul(pf, lh, rl, start=False, stop=False)
                    last = e.matmul(pf, ll, rh, start=False, stop=(kc == KC - 1))
                return last
            g.op("tensor", mm,
                 reads=[f"wpair{j}"] + [f"xhf{t}" for t in range(TB)],
                 writes=[pf_res])
            epi(mo, pf, pf_res)

    def gelu_epi(mo, pf, pf_res):
        hsl = slice(mo * TT, (mo + 1) * TT)
        g.op("scalar", lambda e, pf=pf, mo=mo: e.activation(
            g32, pf, AF.Gelu, bias=biasfc[:, mo:mo + 1]),
            reads=[pf_res, "biasfc"], writes=["g32", pf_res])
        g.op("vector", lambda e, hsl=hsl: (
            e.tensor_copy(h_hi[:, hsl], g32),
            e.tensor_tensor(h_lo[:, hsl], g32, h_hi[:, hsl], AL.subtract))[-1],
            reads=["g32"], writes=[f"h{mo}"])

    def quant_epi(mo, pf, pf_res):
        hsl = slice(mo * TT, (mo + 1) * TT)
        g.op("scalar", lambda e, pf=pf, mo=mo: (
            e.activation(scr[:, 0:TT], pf, AF.Sin, scale=0.1,
                         bias=b1sin[:, mo:mo + 1]),
            e.activation(scr[:, TT:D], pf, AF.Sin, scale=0.1,
                         bias=b1cos[:, mo:mo + 1]))[-1],
            reads=[pf_res, "b1sin", "b1cos"], writes=["scr"])
        def qv(e, pf=pf, mo=mo, hsl=hsl):
            e.tensor_scalar(hbuf, pf, b1f_s[:, mo:mo + 1], None, AL.add)
            e.scalar_tensor_tensor(g32, scr[:, TT:D], P1[:, mo:mo + 1], hbuf,
                                   AL.mult, AL.add)
            e.scalar_tensor_tensor(g32, scr[:, 0:TT], P2[:, mo:mo + 1], g32,
                                   AL.mult, AL.add)
            e.tensor_scalar(h_hi[:, hsl], g32, 0.0, None, AL.max)
            return e.scalar_tensor_tensor(h_lo[:, hsl], g32, 0.0, h_hi[:, hsl],
                                          AL.max, AL.subtract)
        g.op("vector", qv,
             reads=[pf_res, "scr", "b1f_s", "P1", "P2"],
             writes=[f"h{mo}", "hbuf", "g32", pf_res])

    def hproj_phase(wname, nm):
        load_bias_rows(nm)
        for tbs in ((0, 1), (2, 3)):
            gis = [tb * 2 + nb for tb in tbs for nb in range(2)]
            wr = sorted({GR[gi] for gi in gis})

            def bias_mm(e, gis=gis):
                last = None
                for gi in gis:
                    nb = gi % 2
                    last = e.matmul(G[gi], ones_r,
                                    brows2[:, nb * TT:(nb + 1) * TT],
                                    start=True, stop=False)
                return last
            g.op("tensor", bias_mm, reads=["browh", "browl", "ones_r"], writes=wr)
            for mo in range(MO):
                j = mo % NWP
                g.op("sync", lambda e, mo=mo, j=j: e.dma_start(
                    out=wpair[j], in_=d_wscr[wname][mo * 128:(mo + 1) * 128, :]),
                    reads=[f"wscr_{wname}"], writes=[f"wpair{j}"], inc=16)

                def mm(e, mo=mo, j=j, gis=gis):
                    last = None
                    for gi in gis:
                        tb, nb = gi // 2, gi % 2
                        lh = h_hi[:, mo * TT + tb * 128: mo * TT + (tb + 1) * 128]
                        ll = h_lo[:, mo * TT + tb * 128: mo * TT + (tb + 1) * 128]
                        rh = wpair[j][:, nb * TT:(nb + 1) * TT]
                        rl = wpair[j][:, D + nb * TT:D + (nb + 1) * TT]
                        e.matmul(G[gi], lh, rh, start=False, stop=False)
                        e.matmul(G[gi], lh, rl, start=False, stop=False)
                        last = e.matmul(G[gi], ll, rh, start=False,
                                        stop=(mo == MO - 1))
                    return last
                g.op("tensor", mm, reads=[f"wpair{j}", f"h{mo}"], writes=wr)
            for tb in tbs:
                resid_evac(tb)

    for tile in range(n_tiles):
        ph = 0
        for tb in range(TB):
            r0 = tile * TT + tb * 128
            g.op("gpsimd", lambda e, tb=tb, r0=r0: e.dma_start(
                out=x_t[tb], in_=d_x[r0:r0 + 128, :]),
                writes=[f"x{tb}"], inc=16)
            g.op("gpsimd", lambda e, tb=tb, r0=r0: e.dma_start(
                out=x0_t[tb], in_=d_x[r0:r0 + 128, :]),
                writes=[f"xb{tb}"], inc=16)
        ph += 1
        if ph >= upto: break
        ln_stage(tile)
        ph += 1
        if ph >= upto: break
        load_bias_rows("bq")
        for tbs in ((0, 1), (2, 3)):
            tok_proj_phase("wq", xhf_hi, xhf_lo,
                           [f"xhf{t}" for t in tbs], tbs=tbs)
            for tb in tbs:
                elu_evac(tb, fq[:, tb * D:(tb + 1) * D], f"fq{tb}")
        ph += 1
        if ph >= upto: break
        load_bias_rows("bk")
        for tbs in ((0, 1), (2, 3)):
            tok_proj_phase("wk", xhf_hi, xhf_lo,
                           [f"xhf{t}" for t in tbs], tbs=tbs)
            for tb in tbs:
                elu_evac(tb, fkb, "fkb")
                stage_A(tb)
        ph += 1
        if ph >= upto: break
        load_bias_rows("bv")
        def attn_tr(tb):
            transpose_pair(a_hi[tb], a_lo[tb], [f"a{tb}"],
                           lambda tb=tb: (
                               a_hi[tb].rearrange("p (a b) -> p a b", a=KC),
                               a_lo[tb].rearrange("p (a b) -> p a b", a=KC)),
                           [f"a{tb}"], slot=(2, 3, 1, 0)[tb])
        tok_proj_phase("wv", xhf_hi, xhf_lo, ["xhf0"], tbs=(0,))
        copy_evac(0, fvb, "fvb")
        tok_proj_phase("wv", xhf_hi, xhf_lo, ["xhf1"], tbs=(1,))
        stage_2(0)
        copy_evac(1, fvb, "fvb")
        tok_proj_phase("wv", xhf_hi, xhf_lo, ["xhf2"], tbs=(2,))
        stage_2(1)
        attn_tr(0)
        copy_evac(2, fvb, "fvb")
        tok_proj_phase("wv", xhf_hi, xhf_lo, ["xhf3"], tbs=(3,))
        stage_2(2)
        attn_tr(1)
        copy_evac(3, fvb, "fvb")
        ph += 1
        if ph >= upto: break
        ph += 1
        if ph >= upto: break
        load_bias_rows("bo")
        wo_hi = lambda kc, tb: a_hi[tb][:, kc * 128:(kc + 1) * 128]
        wo_lo = lambda kc, tb: a_lo[tb][:, kc * 128:(kc + 1) * 128]
        tok_proj_phase("wo", wo_hi, wo_lo, ["a0"], tbs=(0,))
        stage_2(3)
        attn_tr(2)
        tok_proj_phase("wo", wo_hi, wo_lo, ["a1"], tbs=(1,))
        attn_tr(3)
        resid_evac(0)
        tok_proj_phase("wo", wo_hi, wo_lo, ["a2"], tbs=(2,))
        resid_evac(1)
        resid_evac(2)
        tok_proj_phase("wo", wo_hi, wo_lo, ["a3"], tbs=(3,))
        resid_evac(3)
        ph += 1
        if ph >= upto: break
        ln_stage(tile, slots=(2, 3, 0, 3))
        ph += 1
        if ph >= upto: break
        mlp_phase("fcw", g2_s, gelu_epi)
        ph += 1
        if ph >= upto: break
        hproj_phase("projw", "bpr")
        ph += 1
        if ph >= upto: break
        for tb in range(TB):
            g.op("vector", lambda e, tb=tb: e.tensor_tensor(
                x_t[tb], x_t[tb], x0_t[tb], AL.add),
                reads=[f"xb{tb}"], writes=[f"x{tb}"])
        ln_stage(tile, replace=True, slots=(2, 3, 0, 3))
        ph += 1
        if ph >= upto: break
        mlp_phase("w1w", None, quant_epi)
        ph += 1
        if ph >= upto: break
        hproj_phase("w2w", "bw2")
        ph += 1
        if ph >= upto: break
        ln_stage(tile, final=True)

    for nm, ap, reads in taps:
        dtap = nc.dram_tensor("tap_" + nm, list(ap.shape), ap.dtype,
                              kind="ExternalOutput").ap()
        g.op("gpsimd", lambda e, dtap=dtap, ap=ap: e.dma_start(out=dtap, in_=ap),
             reads=list(reads), inc=16)

    stack = contextlib.ExitStack()
    with stack:
        engines, mk = g.emit(nc, stack)
        with nc.Block() as block:
            for e in engines:
                getattr(block, e)(mk(e))
    return nc


def stage_inputs(inp):
    f32 = lambda a: np.ascontiguousarray(np.asarray(a, dtype=np.float32))
    x = f32(inp["L0_attn_q_x"]).reshape(NCORES * TOKENS, D)
    com = {
        "wq": f32(inp["L0_attn_wq"]),
        "wk": f32(inp["L0_attn_wk"]),
        "wv": f32(inp["L0_attn_wv"]),
        "wo": f32(inp["L0_attn_wo"]),
        "fcw": f32(np.asarray(inp["L0_attn_fc_w"], np.float32)
                   .reshape(KC, 128, MO, 128).transpose(2, 1, 0, 3).reshape(MLP, D)),
        "projw": f32(inp["L0_attn_proj_w"]),
        "w1w": f32(np.asarray(inp["L0_ffn_W1"], np.float32)
                   .reshape(KC, 128, MO, 128).transpose(2, 1, 0, 3).reshape(MLP, D)),
        "w2w": f32(inp["L0_ffn_W2"]),
        "g1": f32(np.asarray(inp["L0_attn_ln1_w"], np.float32).reshape(KC, 128).T),
        "be1": f32(np.asarray(inp["L0_attn_ln1_b"], np.float32).reshape(KC, 128).T),
        "g2": f32(np.asarray(inp["L0_attn_ln2_w"], np.float32).reshape(KC, 128).T),
        "be2": f32(np.asarray(inp["L0_attn_ln2_b"], np.float32).reshape(KC, 128).T),
        "fcb": f32(np.asarray(inp["L0_attn_fc_b"], np.float32).reshape(MO, 128).T),
        "b1f": f32(np.asarray(inp["L0_ffn_b1"], np.float32).reshape(MO, 128).T),
        "th": f32(np.asarray(inp["L0_ffn_quantum_phases"], np.float32).reshape(MO, 128).T),
        "wr": f32(np.asarray(inp["L0_ffn_quantum_weights_real"], np.float32).reshape(MO, 128).T),
        "wi": f32(np.asarray(inp["L0_ffn_quantum_weights_imag"], np.float32).reshape(MO, 128).T),
        "bq": f32(inp["L0_attn_bq"]).reshape(1, D),
        "bk": f32(inp["L0_attn_bk"]).reshape(1, D),
        "bv": f32(inp["L0_attn_bv"]).reshape(1, D),
        "bo": f32(inp["L0_attn_bo"]).reshape(1, D),
        "bpr": f32(inp["L0_attn_proj_b"]).reshape(1, D),
        "bw2": f32(inp["L0_ffn_b2"]).reshape(1, D),
        "eye": np.eye(128, dtype=ml_dtypes.bfloat16),
    }
    in_maps = []
    for c in range(NCORES):
        m = dict(com)
        m["x"] = np.ascontiguousarray(x[c * TOKENS:(c + 1) * TOKENS])
        in_maps.append(m)
    return in_maps


def kernel(**inputs):
    in_maps = stage_inputs(inputs)
    nc = build_nc()
    res = run_bass_kernel_spmd(nc, in_maps, core_ids=list(range(NCORES)))
    y = np.concatenate([np.asarray(res.results[c]["y"]) for c in range(NCORES)],
                       axis=0)
    return np.ascontiguousarray(y.reshape(8, 4096, 1024), dtype=np.float32)


# version 12
# speedup vs baseline: 1.0391x; 1.0391x over previous
import contextlib
import numpy as np
import ml_dtypes
import concourse.bass as bass
import concourse.mybir as mybir
from concourse.bass_utils import run_bass_kernel_spmd

F32 = mybir.dt.float32
BF16 = mybir.dt.bfloat16
AL = mybir.AluOpType
AF = mybir.ActivationFunctionType
AX = mybir.AxisListType.X

NCORES = 8
TOKENS = 4096
TT = 512
NT = TOKENS // TT
TB = TT // 128
D, KC, MLP, MO, H, HD = 1024, 8, 4096, 32, 16, 64
HALF_PI = float(np.float32(np.pi / 2))
TWO_PI_HI = float(np.float32(2 * np.pi))
TWO_PI_LO = float(np.float32(2 * np.pi - np.float64(np.float32(2 * np.pi))))
PI = float(np.pi)
PI_CLAMP = 3.1415925


SERIALIZE = False


class Prog:

    def __init__(self):
        self.ops = []
        self.writer = {}
        self.readers = {}

    def op(self, engine, fn, reads=(), writes=(), inc=1, drain=False):
        drain = drain or engine in ("vector", "scalar")
        if SERIALIZE == 1:
            reads = list(reads) + ["SERIAL"]
            writes = list(writes) + ["SERIAL"]
        elif SERIALIZE == 2 and engine == "sync":
            reads = list(reads) + ["SERIAL"]
            writes = list(writes) + ["SERIAL"]
        elif SERIALIZE == 3 and engine == "tensor":
            reads = list(reads) + ["SERIAL"]
            writes = list(writes) + ["SERIAL"]
        elif SERIALIZE == 4 and engine in ("vector", "scalar"):
            reads = list(reads) + ["SERIAL"]
            writes = list(writes) + ["SERIAL"]
        i = len(self.ops)
        deps = set()
        for r in reads:
            w = self.writer.get(r)
            if w is not None:
                deps.add(w)
        for r in writes:
            w = self.writer.get(r)
            if w is not None:
                deps.add(w)
            deps.update(self.readers.get(r, ()))
        self.ops.append({"e": engine, "fn": fn, "deps": deps, "inc": inc,
                         "drain": drain})
        for r in reads:
            self.readers.setdefault(r, []).append(i)
        for r in writes:
            self.writer[r] = i
            self.readers[r] = []
        return i

    def emit(self, nc, stack):
        engines = []
        for o in self.ops:
            if o["e"] not in engines:
                engines.append(o["e"])
        ref = set()
        for o in self.ops:
            for d in o["deps"]:
                if self.ops[d]["e"] != o["e"]:
                    ref.add(d)
        for i, o in enumerate(self.ops):
            if o["e"] == "gpsimd" and o["inc"] == 16:
                ref.add(i)
        NPOOL = 8
        pool_engines = {o["e"] for o in self.ops if o["inc"] == 16}
        sems = {}
        pools = {}
        for e in engines:
            sems[e] = stack.enter_context(nc.semaphore("sem_" + e))
            if e in pool_engines:
                pools[e] = [stack.enter_context(nc.semaphore(f"dmap_{e}{k}"))
                            for k in range(NPOOL)]
        cum = {e: 0 for e in engines}
        pcount = {e: [0] * NPOOL for e in pool_engines}
        pidx = {e: 0 for e in pool_engines}
        semkey = [None] * len(self.ops)
        for i, o in enumerate(self.ops):
            e = o["e"]
            if o["inc"] == 16:
                k = pidx[e]
                pidx[e] = (k + 1) % NPOOL
                pcount[e][k] += 16
                semkey[i] = (("pool", e, k), pcount[e][k])
            else:
                if i in ref:
                    cum[e] += o["inc"]
                semkey[i] = (("eng", e), cum[e])

        def sem_of(key):
            return pools[key[1]][key[2]] if key[0] == "pool" else sems[key[1]]

        final_gp = dict(pcount.get("gpsimd", {} and None) and {}) if False else None
        gp_final = (list(enumerate(pcount["gpsimd"]))
                    if "gpsimd" in pcount else [])

        def mk(e):
            def emit_engine(eng):
                wm = {}
                for i, o in enumerate(self.ops):
                    if o["e"] != e:
                        continue
                    for d in sorted(o["deps"]):
                        de = self.ops[d]["e"]
                        if de == e:
                            continue
                        key, v = semkey[d]
                        if wm.get(key, 0) < v:
                            eng.wait_ge(sem_of(key), v)
                            wm[key] = v
                    inst = o["fn"](eng)
                    if o["drain"]:
                        inst = eng.drain()
                    if o["inc"] == 16 or i in ref:
                        key, _ = semkey[i]
                        inst.then_inc(sem_of(key), o["inc"])
                if e == "gpsimd":
                    for k, v in gp_final:
                        if v:
                            eng.wait_ge(pools["gpsimd"][k], v)
            return emit_engine

        return engines, mk


def _bc(ap, dim, count):
    lst = [list(p) for p in ap.ap]
    assert lst[dim][1] == 1
    lst[dim] = [0, count]
    return bass.AP(ap.tensor, ap.offset, lst)


def _swap_free(ap3):
    lst = [list(p) for p in ap3.ap]
    return bass.AP(ap3.tensor, ap3.offset, [lst[0], lst[2], lst[1]])


def build_nc(n_tiles=NT, upto=99, taps=()):
    nc = bass.Bass("TRN2", target_bir_lowering=False)

    d_x = nc.dram_tensor("x", [TOKENS, D], F32, kind="ExternalInput").ap()
    d_w = {}
    for nm in ("wq", "wk", "wv", "wo"):
        d_w[nm] = nc.dram_tensor(nm, [D, D], F32, kind="ExternalInput").ap()
    for nm in ("fcw", "projw", "w1w", "w2w"):
        d_w[nm] = nc.dram_tensor(nm, [MLP, D], F32, kind="ExternalInput").ap()
    d_g1 = nc.dram_tensor("g1", [128, KC], F32, kind="ExternalInput").ap()
    d_be1 = nc.dram_tensor("be1", [128, KC], F32, kind="ExternalInput").ap()
    d_g2 = nc.dram_tensor("g2", [128, KC], F32, kind="ExternalInput").ap()
    d_be2 = nc.dram_tensor("be2", [128, KC], F32, kind="ExternalInput").ap()
    d_fcb = nc.dram_tensor("fcb", [128, MO], F32, kind="ExternalInput").ap()
    d_b1f = nc.dram_tensor("b1f", [128, MO], F32, kind="ExternalInput").ap()
    d_th = nc.dram_tensor("th", [128, MO], F32, kind="ExternalInput").ap()
    d_wrr = nc.dram_tensor("wr", [128, MO], F32, kind="ExternalInput").ap()
    d_wii = nc.dram_tensor("wi", [128, MO], F32, kind="ExternalInput").ap()
    d_brow = {nm: nc.dram_tensor(nm, [1, D], F32, kind="ExternalInput").ap()
              for nm in ("bq", "bk", "bv", "bo", "bpr", "bw2")}
    d_eye = nc.dram_tensor("eye", [128, 128], BF16, kind="ExternalInput").ap()
    d_y = nc.dram_tensor("y", [TOKENS, D], F32, kind="ExternalOutput").ap()
    d_rows = nc.dram_tensor("rows_scr", [12, D], BF16, kind="Internal").ap()
    d_wscr = {}
    for nm in ("wq", "wk", "wv", "wo"):
        d_wscr[nm] = nc.dram_tensor("scr_" + nm, [D, 2 * D], BF16, kind="Internal").ap()
    for nm in ("fcw", "projw", "w1w", "w2w"):
        d_wscr[nm] = nc.dram_tensor("scr_" + nm, [MLP, 2 * D], BF16, kind="Internal").ap()

    sb = lambda nm, sh, dt=F32: nc.alloc_sbuf_tensor(nm, sh, dt).ap()
    x_t = [sb(f"x{t}", [128, D]) for t in range(TB)]
    x0_t = [sb(f"xb{t}", [128, D]) for t in range(TB)]
    xhf = [sb(f"xhf{i}", [128, KC * TT], BF16) for i in range(2)]
    a_hi = [sb(f"ahi{t}", [128, D], BF16) for t in range(TB)]
    a_lo = [sb(f"alo{t}", [128, D], BF16) for t in range(TB)]
    xh_tm2 = [sb(f"xh_tm{i}", [128, D], BF16) for i in range(2)]
    xl_tm2 = [sb(f"xl_tm{i}", [128, D], BF16) for i in range(2)]
    jnk_tm = sb("jnk_tm", [128, D], BF16)
    fq = sb("fq", [128, TB * D])
    fkb = sb("fkb", [128, D])
    fvb = sb("fvb", [128, D])
    qkvb = sb("qkvb", [128, D])
    Ab = sb("Ab", [128, TB * H * H])
    qks = sb("qks", [128, H])
    rcp = sb("rcp", [128, H])
    scr = sb("scr", [128, D])
    g32 = sb("g32", [128, TT])
    hbuf = sb("hbuf", [128, TT])
    h_hi = sb("h_hi", [128, MO * TT], BF16)
    h_lo = sb("h_lo", [128, MO * TT], BF16)
    wbuf = [sb("wbuf0", [128, D])]
    wbh = [sb(f"wbh{i}", [128, D], BF16) for i in range(2)]
    wbl = [sb(f"wbl{i}", [128, D], BF16) for i in range(2)]
    NWP = 3
    wpair = [sb(f"wpair{i}", [128, 2 * D], BF16) for i in range(NWP)]
    stat = sb("stat", [128, 3 * TB])
    eye_s = sb("eye_s", [128, 128], BF16)
    ones_r = sb("ones_r", [2, 128], BF16)
    g1_s = sb("g1_s", [128, KC]); be1_s = sb("be1_s", [128, KC])
    g2_s = sb("g2_s", [128, KC]); be2_s = sb("be2_s", [128, KC])
    fcb_s = sb("fcb_s", [128, MO]); b1f_s = sb("b1f_s", [128, MO])
    th_s = sb("th_s", [128, MO]); wr_s = sb("wr_s", [128, MO]); wi_s = sb("wi_s", [128, MO])
    biasfc = sb("biasfc", [128, MO])
    b1sin = sb("b1sin", [128, MO]); b1cos = sb("b1cos", [128, MO])
    P1 = sb("P1", [128, MO]); P2 = sb("P2", [128, MO])
    sinb = sb("sinb", [128, MO]); cosb = sb("cosb", [128, MO])
    qtmp = sb("qtmp", [128, MO]); qmask = sb("qmask", [128, MO]); qpb = sb("qpb", [128, MO])
    brow32 = scr[0:1, :]
    brows2 = sb("brows2", [2, D], BF16)
    browh = brows2[0:1, :]
    browl = brows2[1:2, :]

    pA = nc.alloc_psum_tensor("pA", [128, D], F32).ap()
    pB = nc.alloc_psum_tensor("pB", [128, D], F32).ap()
    pC = nc.alloc_psum_tensor("pC", [128, TT], F32).ap()
    pD = nc.alloc_psum_tensor("pD", [128, TT], F32).ap()
    pE = nc.alloc_psum_tensor("pE", [128, TT], F32).ap()
    pF = nc.alloc_psum_tensor("pF", [128, TT], F32).ap()
    pEb = pE.bitcast(BF16)
    pFb = pF.bitcast(BF16)
    G = [pA[:, 0:TT], pA[:, TT:D], pB[:, 0:TT], pB[:, TT:D], pC, pD, pE, pF]
    GR = ["pA", "pA", "pB", "pB", "pC", "pD", "pE", "pF"]
    ALLG = ["pA", "pB", "pC", "pD", "pE", "pF"]

    g = Prog()

    for dst, src in ((g1_s, d_g1), (be1_s, d_be1), (g2_s, d_g2), (be2_s, d_be2),
                     (fcb_s, d_fcb), (b1f_s, d_b1f), (th_s, d_th), (wr_s, d_wrr),
                     (wi_s, d_wii), (eye_s, d_eye)):
        g.op("gpsimd", lambda e, a=dst, b=src: e.dma_start(out=a, in_=b),
             writes=[dst.tensor.name], inc=16)
    g.op("vector", lambda e: e.memset(ones_r, 1.0), writes=["ones_r"], drain=True)

    def wrap_ladder(e, shift):
        e.tensor_scalar(qtmp, th_s, shift, 0.0, AL.add, AL.add)
        e.drain()
        for _ in range(8):
            e.tensor_scalar(qmask, qtmp, PI, 0.0, AL.is_gt, AL.add)
            e.drain()
            e.scalar_tensor_tensor(qtmp, qmask, -TWO_PI_HI, qtmp, AL.mult, AL.add)
            e.drain()
            e.scalar_tensor_tensor(qtmp, qmask, -TWO_PI_LO, qtmp, AL.mult, AL.add)
            e.drain()
            e.tensor_scalar(qmask, qtmp, -PI, 0.0, AL.is_lt, AL.add)
            e.drain()
            e.scalar_tensor_tensor(qtmp, qmask, TWO_PI_HI, qtmp, AL.mult, AL.add)
            e.drain()
            e.scalar_tensor_tensor(qtmp, qmask, TWO_PI_LO, qtmp, AL.mult, AL.add)
            e.drain()
        return e.tensor_scalar(qtmp, qtmp, PI_CLAMP, -PI_CLAMP, AL.min, AL.max)

    g.op("vector", lambda e: wrap_ladder(e, 0.0),
         reads=["th_s"], writes=["qtmp", "qmask"], drain=True)
    g.op("scalar", lambda e: e.activation(sinb, qtmp, AF.Sin),
         reads=["qtmp"], writes=["sinb"], drain=True)
    g.op("vector", lambda e: wrap_ladder(e, HALF_PI),
         reads=["th_s", "sinb"], writes=["qtmp", "qmask"], drain=True)
    g.op("scalar", lambda e: e.activation(cosb, qtmp, AF.Sin),
         reads=["qtmp"], writes=["cosb"], drain=True)

    def qfin(e):
        e.tensor_tensor(qpb, cosb, wr_s, AL.mult)
        e.tensor_tensor(qtmp, sinb, wi_s, AL.mult)
        e.drain()
        e.tensor_tensor(qpb, qpb, qtmp, AL.add)
        e.drain()
        e.tensor_tensor(qpb, qpb, wi_s, AL.mult)
        e.drain()
        e.tensor_scalar(qpb, qpb, 0.1, 0.0, AL.mult, AL.add)
        e.drain()
        e.tensor_tensor(P1, qpb, sinb, AL.mult)
        e.tensor_tensor(P2, qpb, cosb, AL.mult)
        e.tensor_scalar(b1sin, b1f_s, 0.1, 0.0, AL.mult, AL.add)
        return e.tensor_scalar(b1cos, b1f_s, 0.1, HALF_PI, AL.mult, AL.add)
    g.op("vector", qfin,
         reads=["sinb", "cosb", "wr_s", "wi_s", "b1f_s"],
         writes=["P1", "P2", "qpb", "qtmp", "b1sin", "b1cos"], drain=True)

    def prep_weight_slow(wname):
        nrows = d_w[wname].shape[0]
        for c in range(nrows // 128):
            g.op("sync", lambda e, c=c, wname=wname: e.dma_start(
                out=wbuf[0], in_=d_w[wname][c * 128:(c + 1) * 128, :]),
                writes=["wbuf0"], inc=16)
            def cast_plain(e):
                e.tensor_copy(wbh[0], wbuf[0])
                e.drain()
                return e.tensor_tensor(wbl[0], wbuf[0], wbh[0], AL.subtract)
            g.op("vector", cast_plain, reads=["wbuf0"], writes=["wbh0", "wbl0"])
            g.op("sync", lambda e, c=c, wname=wname: e.dma_start(
                out=d_wscr[wname][c * 128:(c + 1) * 128, 0:D], in_=wbh[0]),
                reads=["wbh0"], writes=[f"wscr_{wname}"], inc=16)
            g.op("sync", lambda e, c=c, wname=wname: e.dma_start(
                out=d_wscr[wname][c * 128:(c + 1) * 128, D:2 * D], in_=wbl[0]),
                reads=["wbl0"], writes=[f"wscr_{wname}"], inc=16)

    def prep_weight(wname, gamma=None, beta=None, bias_col_psum=False):
        nrows = d_w[wname].shape[0]
        nch = nrows // 128
        for c in range(nch):
            j = c % 2
            win = wpair[j].bitcast(F32)
            whi = wpair[2][:, 0:D]
            wlo = wpair[2][:, D:2 * D]
            g.op("sync", lambda e, c=c, win=win, wname=wname: e.dma_start(
                out=win, in_=d_w[wname][c * 128:(c + 1) * 128, :]),
                writes=[f"wpair{j}"], inc=16)
            if gamma is None:
                def cast_plain(e, win=win, whi=whi, wlo=wlo):
                    e.tensor_copy(whi, win)
                    e.drain()
                    return e.tensor_tensor(wlo, win, whi, AL.subtract)
                g.op("vector", cast_plain, reads=[f"wpair{j}"],
                     writes=["wpair2"])
            elif gamma == "bcast":
                def cast_fold2(e, win=win, whi=whi, wlo=wlo):
                    gb = _bc(g2_s.rearrange("p (a b) -> p a b", b=1), 2, 128)
                    w3 = win.rearrange("p (a b) -> p a b", a=KC)
                    e.tensor_tensor(scr.rearrange("p (a b) -> p a b", a=KC),
                                    w3, gb, AL.mult)
                    e.drain()
                    e.tensor_copy(whi, scr)
                    e.drain()
                    return e.tensor_tensor(wlo, scr, whi, AL.subtract)
                g.op("vector", cast_fold2,
                     reads=[f"wpair{j}", "g2_s"], writes=["wpair2", "scr"])
            else:
                def cast_gam(e, c=c, win=win, whi=whi, wlo=wlo, gamma=gamma):
                    e.tensor_scalar(whi, win, gamma[:, c % KC:c % KC + 1], 0.0,
                                    AL.mult, AL.add)
                    e.drain()
                    return e.scalar_tensor_tensor(
                        wlo, win, gamma[:, c % KC:c % KC + 1], whi,
                        AL.mult, AL.subtract)
                g.op("vector", cast_gam,
                     reads=[f"wpair{j}", gamma.tensor.name], writes=["wpair2"])
            g.op("sync", lambda e, c=c, wname=wname: e.dma_start(
                out=d_wscr[wname][c * 128:(c + 1) * 128, :], in_=wpair[2]),
                reads=["wpair2"], writes=[f"wscr_{wname}"], inc=16)
            if beta is not None and not bias_col_psum:
                g.op("tensor", lambda e, c=c, win=win: (
                    e.matmul(pC[0:1, :], beta[:, c:c + 1], win[:, 0:TT],
                             start=(c == 0), stop=(c == KC - 1)),
                    e.matmul(pD[0:1, :], beta[:, c:c + 1], win[:, TT:D],
                             start=(c == 0), stop=(c == KC - 1)))[-1],
                    reads=[f"wpair{j}", beta.tensor.name], writes=["pC", "pD"])
            if beta is not None and bias_col_psum:
                g.op("tensor", lambda e, c=c, win=win: [
                    e.matmul(pC[:, c:c + 1], win[:, kc * 128:(kc + 1) * 128],
                             beta[:, kc:kc + 1], start=(kc == 0),
                             stop=(kc == KC - 1))
                    for kc in range(KC)][-1],
                    reads=[f"wpair{j}", beta.tensor.name], writes=["pC"])

    ROWIDX = {"bq": 0, "bk": 1, "bv": 2, "bo": 3, "bpr": 4, "bw2": 5}

    def prep_bias_row(nm, with_corr=False):
        r = ROWIDX[nm]
        g.op("gpsimd", lambda e: e.dma_start(out=brow32, in_=d_brow[nm]),
             writes=["brow32", "scr"], inc=16)
        if with_corr:
            g.op("vector", lambda e: (
                e.tensor_tensor(brow32[:, 0:TT], pC[0:1, :], brow32[:, 0:TT], AL.add),
                e.tensor_tensor(brow32[:, TT:D], pD[0:1, :], brow32[:, TT:D], AL.add))[-1],
                reads=["pC", "pD"], writes=["brow32", "scr", "pC", "pD"], drain=True)
        hi_st = qkvb[0:1, 0:TT].bitcast(BF16)
        lo_st = qkvb[0:1, TT:D].bitcast(BF16)
        def row_fin(e):
            e.tensor_copy(hi_st, brow32)
            e.drain()
            return e.tensor_tensor(lo_st, brow32, hi_st, AL.subtract)
        g.op("vector", row_fin, reads=["brow32", "scr"],
             writes=["rowstage", "qkvb"], drain=True)
        g.op("gpsimd", lambda e, r=r: e.dma_start(out=d_rows[r:r + 1, :], in_=hi_st),
             reads=["rowstage"], writes=[f"rows{r}"], inc=16)
        g.op("gpsimd", lambda e, r=r: e.dma_start(out=d_rows[6 + r:7 + r, :], in_=lo_st),
             reads=["rowstage"], writes=[f"rows{6 + r}"], inc=16)

    prep_weight("wq", gamma=g1_s, beta=be1_s)
    prep_bias_row("bq", with_corr=True)
    prep_weight("wk", gamma=g1_s, beta=be1_s)
    prep_bias_row("bk", with_corr=True)
    prep_weight("wv", gamma=g1_s, beta=be1_s)
    prep_bias_row("bv", with_corr=True)
    prep_weight("fcw", gamma="bcast", beta=be2_s, bias_col_psum=True)
    g.op("vector", lambda e: e.tensor_tensor(biasfc, pC[:, 0:MO], fcb_s, AL.add),
         reads=["pC", "fcb_s"], writes=["biasfc", "pC"], drain=True)
    prep_bias_row("bo")
    prep_bias_row("bpr")
    prep_bias_row("bw2")
    prep_weight("wo")
    prep_weight("projw")
    prep_weight("w1w")
    prep_weight("w2w")

    def load_bias_rows(nm):
        r = ROWIDX[nm]
        g.op("sync", lambda e, r=r: e.dma_start(out=browh, in_=d_rows[r:r + 1, :]),
             reads=[f"rows{r}"], writes=["browh"], inc=16)
        g.op("sync", lambda e, r=r: e.dma_start(out=browl, in_=d_rows[6 + r:7 + r, :]),
             reads=[f"rows{6 + r}"], writes=["browl"], inc=16)

    def tok_proj_phase(wname, lhs_hi_fn, lhs_lo_fn, lhs_reads, tbs=(0, 1, 2, 3)):
        gis = [tb * 2 + nb for tb in tbs for nb in range(2)]
        wr = sorted({GR[gi] for gi in gis})
        for kc in range(KC):
            j = (kc + (0 if tbs[0] == 0 else KC)) % NWP if False else kc % NWP
            g.op("sync", lambda e, kc=kc, j=j: e.dma_start(
                out=wpair[j], in_=d_wscr[wname][kc * 128:(kc + 1) * 128, :]),
                reads=[f"wscr_{wname}"], writes=[f"wpair{j}"], inc=16)

            def mm(e, kc=kc, j=j):
                last = None
                if kc == 0:
                    for gi in gis:
                        nb = gi % 2
                        e.matmul(G[gi], ones_r, brows2[:, nb * TT:(nb + 1) * TT],
                                 start=True, stop=False)
                for gi in gis:
                    tb, nb = gi // 2, gi % 2
                    lh = lhs_hi_fn(kc, tb)
                    ll = lhs_lo_fn(kc, tb)
                    rh = wpair[j][:, nb * TT:(nb + 1) * TT]
                    rl = wpair[j][:, D + nb * TT:D + (nb + 1) * TT]
                    e.matmul(G[gi], lh, rh, start=False, stop=False)
                    e.matmul(G[gi], lh, rl, start=False, stop=False)
                    last = e.matmul(G[gi], ll, rh, start=False, stop=(kc == KC - 1))
                return last
            rds = [f"wpair{j}", "ones_r"] + lhs_reads
            if kc == 0:
                rds += ["browh", "browl"]
            g.op("tensor", mm, reads=rds, writes=wr)

    def xhf_hi(kc, tb):
        return xhf[0][:, kc * TT + tb * 128: kc * TT + (tb + 1) * 128]

    def xhf_lo(kc, tb):
        return xhf[1][:, kc * TT + tb * 128: kc * TT + (tb + 1) * 128]

    def elu_evac(tb, dst, dst_res):
        g0, g1 = G[tb * 2], G[tb * 2 + 1]
        r0, r1 = GR[tb * 2], GR[tb * 2 + 1]
        g.op("vector", lambda e: (
            e.tensor_scalar(scr[:, 0:TT], g0, 0.0, None, AL.min),
            e.tensor_scalar(scr[:, TT:D], g1, 0.0, None, AL.min),
            e.tensor_scalar(dst[:, 0:TT], g0, 0.0, None, AL.max),
            e.tensor_scalar(dst[:, TT:D], g1, 0.0, None, AL.max))[-1],
            reads=[r0, r1], writes=["scr", dst_res, r0, r1])
        g.op("scalar", lambda e: e.activation(scr, scr, AF.Exp),
             reads=["scr"], writes=["scr"])
        g.op("vector", lambda e: e.tensor_tensor(dst, dst, scr, AL.add),
             reads=["scr"], writes=[dst_res])

    def copy_evac(tb, dst, dst_res):
        g0, g1 = G[tb * 2], G[tb * 2 + 1]
        r0, r1 = GR[tb * 2], GR[tb * 2 + 1]
        def ev(e, g0=g0, g1=g1):
            dv = dst.rearrange("p (a b) -> p a b", a=HD)
            s0 = bass.AP(g0.tensor, g0.offset, [list(g0.ap[0]), [1, HD], [HD, 8]])
            s1 = bass.AP(g1.tensor, g1.offset, [list(g1.ap[0]), [1, HD], [HD, 8]])
            e.tensor_copy(dv[:, :, 0:8], s0)
            return e.tensor_copy(dv[:, :, 8:16], s1)
        g.op("vector", ev, reads=[r0, r1], writes=[dst_res, r0, r1])

    def resid_evac(tb):
        g0, g1 = G[tb * 2], G[tb * 2 + 1]
        r0, r1 = GR[tb * 2], GR[tb * 2 + 1]
        g.op("vector", lambda e, tb=tb: (
            e.tensor_tensor(x_t[tb][:, 0:TT], g0, x_t[tb][:, 0:TT], AL.add),
            e.tensor_tensor(x_t[tb][:, TT:D], g1, x_t[tb][:, TT:D], AL.add))[-1],
            reads=[r0, r1], writes=[f"x{tb}", r0, r1])

    def stage_A(tb):
        fq_t = fq[:, tb * D:(tb + 1) * D]
        def fn(e, tb=tb):
            fk3 = fkb.rearrange("p (g n) -> p g n", g=H)
            pr3 = scr.rearrange("p (g n) -> p g n", g=H)
            last = None
            for h in range(H):
                fqh = fq_t[:, h * HD:(h + 1) * HD].rearrange("p (g n) -> p g n", g=1)
                e.tensor_tensor(pr3, fk3, _bc(fqh, 1, H), AL.mult)
                last = e.tensor_reduce(
                    Ab[:, tb * H * H + h * H: tb * H * H + (h + 1) * H],
                    pr3, AX, AL.add, opt_output=False)
            return last
        g.op("vector", fn, reads=[f"fq{tb}", "fkb"], writes=[f"A{tb}", "scr"],
             drain=True)

    def stage_2(tb):
        A_t = Ab[:, tb * H * H:(tb + 1) * H * H]
        def fn(e, tb=tb, A_t=A_t):
            e.tensor_reduce(qks, A_t.rearrange("p (g n) -> p g n", g=H), AX,
                            AL.add, opt_output=False)
            e.drain()
            e.tensor_scalar(rcp, qks, 1e-6, 0.0, AL.add, AL.add)
            e.drain()
            e.reciprocal(rcp, rcp)
            e.drain()
            fve = fvb.rearrange("p (a b) -> p a b", a=HD)
            pre = scr.rearrange("p (a b) -> p a b", a=HD)
            for h in range(H):
                Ah = A_t[:, h * H:(h + 1) * H]
                Ab_ = bass.AP(Ah.tensor, Ah.offset, [list(Ah.ap[0]), [0, HD],
                                                     list(Ah.ap[1])])
                e.tensor_tensor(pre, fve, Ab_, AL.mult)
                e.tensor_reduce(qkvb[:, h * HD:(h + 1) * HD],
                                pre, AX, AL.add, opt_output=False)
            e.drain()
            for h in range(H):
                sl = slice(h * HD, (h + 1) * HD)
                e.tensor_scalar(a_hi[tb][:, sl], qkvb[:, sl], rcp[:, h:h + 1],
                                0.0, AL.mult, AL.add)
            last = None
            for h in range(H):
                sl = slice(h * HD, (h + 1) * HD)
                last = e.scalar_tensor_tensor(a_lo[tb][:, sl], qkvb[:, sl],
                                              rcp[:, h:h + 1], a_hi[tb][:, sl],
                                              AL.mult, AL.subtract)
            return last
        g.op("vector", fn, reads=[f"A{tb}", "fvb"],
             writes=[f"a{tb}", "scr", "qkvb", "qks", "rcp"], drain=True)

    pAb = pA.bitcast(BF16)
    pBb = pB.bitcast(BF16)
    pCb = pC.bitcast(BF16)
    pDb = pD.bitcast(BF16)
    TRP = [((pEb, "pE"), (pFb, "pF")),
           ((pCb, "pC"), (pDb, "pD")),
           ((pAb[:, 0:D], "pA"), (pAb[:, D:2 * D], "pA")),
           ((pBb[:, 0:D], "pB"), (pBb[:, D:2 * D], "pB"))]

    def transpose_pair(src_hi, src_lo, reads, dst_view_fn, wr_res, slot=0):
        (ph, phr), (pl, plr) = TRP[slot]
        g.op("tensor", lambda e: ([
            e.transpose(ph[:, b * 128:(b + 1) * 128],
                        src_hi[:, b * 128:(b + 1) * 128], eye_s)
            for b in range(KC)] + [
            e.transpose(pl[:, b * 128:(b + 1) * 128],
                        src_lo[:, b * 128:(b + 1) * 128], eye_s)
            for b in range(KC)])[-1],
            reads=list(reads) + ["eye_s"], writes=[phr, plr])
        def evac(e):
            d0, d1 = dst_view_fn()
            e.tensor_copy(d0, ph.rearrange("p (a b) -> p a b", a=KC))
            return e.tensor_copy(d1, pl.rearrange("p (a b) -> p a b", a=KC))
        g.op("vector", evac, reads=[phr, plr], writes=list(wr_res) + [phr, plr])

    def ln_stage(tile, final=False, replace=False, slots=(2, 3, 0, 1)):
        for tb in range(TB):
            cn, cs = 3 * tb, 3 * tb + 2
            def stats_v(e, tb=tb, cn=cn, cs=cs):
                e.reduce_sum(stat[:, cs:cs + 1], x_t[tb], AX, opt_output=False)
                e.drain()
                return e.tensor_scalar(stat[:, cn:cn + 1], stat[:, cs:cs + 1],
                                       -1.0 / D, 0.0, AL.mult, AL.add)
            g.op("vector", stats_v, reads=[f"x{tb}"], writes=[f"st{tb}"],
                 drain=True)
        for tb in range(TB):
            cn, cs = 3 * tb, 3 * tb + 2
            def stats_s(e, tb=tb, cn=cn, cs=cs):
                return e.activation(jnk_tm, x_t[tb], AF.Square,
                                    bias=stat[:, cn:cn + 1],
                                    accum_out=stat[:, cs:cs + 1])
            g.op("scalar", stats_s, reads=[f"x{tb}", f"st{tb}"],
                 writes=[f"st{tb}", "jnk"], drain=True)
        for tb in range(TB):
            cr, cs = 3 * tb + 1, 3 * tb + 2
            g.op("vector", lambda e, cr=cr, cs=cs: e.tensor_scalar(
                stat[:, cr:cr + 1], stat[:, cs:cs + 1], 1.0 / D, 1e-5,
                AL.mult, AL.add),
                reads=[f"st{tb}"], writes=[f"st{tb}"], drain=True)
        for tb in range(TB):
            cr, cs = 3 * tb + 1, 3 * tb + 2
            def rstd_s(e, cr=cr, cs=cs):
                e.activation(stat[:, cs:cs + 1], stat[:, cr:cr + 1], AF.Ln)
                e.drain()
                return e.activation(stat[:, cr:cr + 1], stat[:, cs:cs + 1],
                                    AF.Exp, scale=-0.5)
            g.op("scalar", rstd_s, reads=[f"st{tb}"], writes=[f"st{tb}"],
                 drain=True)
        if final:
            for tb in range(TB):
                cn, cr = 3 * tb, 3 * tb + 1
                g.op("vector", lambda e, tb=tb, cn=cn, cr=cr: e.tensor_scalar(
                    x_t[tb], x_t[tb], stat[:, cn:cn + 1], stat[:, cr:cr + 1],
                    AL.add, AL.mult),
                    reads=[f"st{tb}"], writes=[f"x{tb}"])
                r0 = tile * TT + tb * 128
                g.op("gpsimd", lambda e, tb=tb, r0=r0: e.dma_start(
                    out=d_y[r0:r0 + 128, :], in_=x_t[tb]),
                    reads=[f"x{tb}"], inc=16)
            return
        for tb in range(TB):
            cn, cr = 3 * tb, 3 * tb + 1
            def apply_v(e, tb=tb, cn=cn, cr=cr, replace=replace):
                e.tensor_scalar(scr, x_t[tb], stat[:, cn:cn + 1],
                                stat[:, cr:cr + 1], AL.add, AL.mult)
                e.tensor_copy(xh_tm2[tb % 2], scr)
                ret = e.tensor_tensor(xl_tm2[tb % 2], scr, xh_tm2[tb % 2],
                                      AL.subtract)
                if replace:
                    ret = e.tensor_copy(x_t[tb], scr)
                return ret
            g.op("vector", apply_v, reads=[f"st{tb}"],
                 writes=[f"xt2_{tb % 2}", "scr", f"x{tb}"])
            def dv(tb=tb):
                v0 = xhf[0].rearrange("p (a b) -> p a b", a=KC)[:, :, tb * 128:(tb + 1) * 128]
                v1 = xhf[1].rearrange("p (a b) -> p a b", a=KC)[:, :, tb * 128:(tb + 1) * 128]
                return v0, v1
            transpose_pair(xh_tm2[tb % 2], xl_tm2[tb % 2], [f"xt2_{tb % 2}"],
                           dv, [f"xhf{tb}"], slot=slots[tb])

    def mlp_phase(wname, gamma, epi):
        for mo in range(MO):
            j = mo % NWP
            g.op("sync", lambda e, mo=mo, j=j: e.dma_start(
                out=wpair[j], in_=d_wscr[wname][mo * 128:(mo + 1) * 128, :]),
                reads=[f"wscr_{wname}"], writes=[f"wpair{j}"], inc=16)
            pf, pf_res = (pC, "pC") if mo % 2 == 0 else (pD, "pD")

            def mm(e, j=j, pf=pf):
                last = None
                for kc in range(KC):
                    lh = wpair[j][:, kc * 128:(kc + 1) * 128]
                    ll = wpair[j][:, D + kc * 128:D + (kc + 1) * 128]
                    rh = xhf[0][:, kc * TT:(kc + 1) * TT]
                    rl = xhf[1][:, kc * TT:(kc + 1) * TT]
                    e.matmul(pf, lh, rh, start=(kc == 0), stop=False)
                    e.matmul(pf, lh, rl, start=False, stop=False)
                    last = e.matmul(pf, ll, rh, start=False, stop=(kc == KC - 1))
                return last
            g.op("tensor", mm,
                 reads=[f"wpair{j}"] + [f"xhf{t}" for t in range(TB)],
                 writes=[pf_res])
            epi(mo, pf, pf_res)

    def gelu_epi(mo, pf, pf_res):
        hsl = slice(mo * TT, (mo + 1) * TT)
        g.op("scalar", lambda e, pf=pf, mo=mo: e.activation(
            g32, pf, AF.Gelu, bias=biasfc[:, mo:mo + 1]),
            reads=[pf_res, "biasfc"], writes=["g32", pf_res])
        g.op("vector", lambda e, hsl=hsl: (
            e.tensor_copy(h_hi[:, hsl], g32),
            e.tensor_tensor(h_lo[:, hsl], g32, h_hi[:, hsl], AL.subtract))[-1],
            reads=["g32"], writes=[f"h{mo}"])

    def quant_epi(mo, pf, pf_res):
        hsl = slice(mo * TT, (mo + 1) * TT)
        g.op("scalar", lambda e, pf=pf, mo=mo: (
            e.activation(scr[:, 0:TT], pf, AF.Sin, scale=0.1,
                         bias=b1sin[:, mo:mo + 1]),
            e.activation(scr[:, TT:D], pf, AF.Sin, scale=0.1,
                         bias=b1cos[:, mo:mo + 1]))[-1],
            reads=[pf_res, "b1sin", "b1cos"], writes=["scr"])
        def qv(e, pf=pf, mo=mo, hsl=hsl):
            e.tensor_scalar(hbuf, pf, b1f_s[:, mo:mo + 1], None, AL.add)
            e.scalar_tensor_tensor(g32, scr[:, TT:D], P1[:, mo:mo + 1], hbuf,
                                   AL.mult, AL.add)
            e.scalar_tensor_tensor(g32, scr[:, 0:TT], P2[:, mo:mo + 1], g32,
                                   AL.mult, AL.add)
            e.tensor_scalar(h_hi[:, hsl], g32, 0.0, None, AL.max)
            return e.scalar_tensor_tensor(h_lo[:, hsl], g32, 0.0, h_hi[:, hsl],
                                          AL.max, AL.subtract)
        g.op("vector", qv,
             reads=[pf_res, "scr", "b1f_s", "P1", "P2"],
             writes=[f"h{mo}", "hbuf", "g32", pf_res])

    def hproj_phase(wname, nm):
        load_bias_rows(nm)
        for tbs in ((0, 1), (2, 3)):
            gis = [tb * 2 + nb for tb in tbs for nb in range(2)]
            wr = sorted({GR[gi] for gi in gis})

            def bias_mm(e, gis=gis):
                last = None
                for gi in gis:
                    nb = gi % 2
                    last = e.matmul(G[gi], ones_r,
                                    brows2[:, nb * TT:(nb + 1) * TT],
                                    start=True, stop=False)
                return last
            g.op("tensor", bias_mm, reads=["browh", "browl", "ones_r"], writes=wr)
            for mo in range(MO):
                j = mo % NWP
                g.op("sync", lambda e, mo=mo, j=j: e.dma_start(
                    out=wpair[j], in_=d_wscr[wname][mo * 128:(mo + 1) * 128, :]),
                    reads=[f"wscr_{wname}"], writes=[f"wpair{j}"], inc=16)

                def mm(e, mo=mo, j=j, gis=gis):
                    last = None
                    for gi in gis:
                        tb, nb = gi // 2, gi % 2
                        lh = h_hi[:, mo * TT + tb * 128: mo * TT + (tb + 1) * 128]
                        ll = h_lo[:, mo * TT + tb * 128: mo * TT + (tb + 1) * 128]
                        rh = wpair[j][:, nb * TT:(nb + 1) * TT]
                        rl = wpair[j][:, D + nb * TT:D + (nb + 1) * TT]
                        e.matmul(G[gi], lh, rh, start=False, stop=False)
                        e.matmul(G[gi], lh, rl, start=False, stop=False)
                        last = e.matmul(G[gi], ll, rh, start=False,
                                        stop=(mo == MO - 1))
                    return last
                g.op("tensor", mm, reads=[f"wpair{j}", f"h{mo}"], writes=wr)
            for tb in tbs:
                resid_evac(tb)

    for tile in range(n_tiles):
        ph = 0
        for tb in range(TB):
            r0 = tile * TT + tb * 128
            g.op("gpsimd", lambda e, tb=tb, r0=r0: e.dma_start(
                out=x_t[tb], in_=d_x[r0:r0 + 128, :]),
                writes=[f"x{tb}"], inc=16)
            g.op("gpsimd", lambda e, tb=tb, r0=r0: e.dma_start(
                out=x0_t[tb], in_=d_x[r0:r0 + 128, :]),
                writes=[f"xb{tb}"], inc=16)
        ph += 1
        if ph >= upto: break
        ln_stage(tile)
        ph += 1
        if ph >= upto: break
        load_bias_rows("bq")
        for tbs in ((0, 1), (2, 3)):
            tok_proj_phase("wq", xhf_hi, xhf_lo,
                           [f"xhf{t}" for t in tbs], tbs=tbs)
            for tb in tbs:
                elu_evac(tb, fq[:, tb * D:(tb + 1) * D], f"fq{tb}")
        ph += 1
        if ph >= upto: break
        load_bias_rows("bk")
        for tbs in ((0, 1), (2, 3)):
            tok_proj_phase("wk", xhf_hi, xhf_lo,
                           [f"xhf{t}" for t in tbs], tbs=tbs)
            for tb in tbs:
                elu_evac(tb, fkb, "fkb")
                stage_A(tb)
        ph += 1
        if ph >= upto: break
        load_bias_rows("bv")
        def attn_tr(tb):
            transpose_pair(a_hi[tb], a_lo[tb], [f"a{tb}"],
                           lambda tb=tb: (
                               a_hi[tb].rearrange("p (a b) -> p a b", a=KC),
                               a_lo[tb].rearrange("p (a b) -> p a b", a=KC)),
                           [f"a{tb}"], slot=(2, 3, 1, 0)[tb])
        tok_proj_phase("wv", xhf_hi, xhf_lo, ["xhf0"], tbs=(0,))
        copy_evac(0, fvb, "fvb")
        tok_proj_phase("wv", xhf_hi, xhf_lo, ["xhf1"], tbs=(1,))
        stage_2(0)
        copy_evac(1, fvb, "fvb")
        tok_proj_phase("wv", xhf_hi, xhf_lo, ["xhf2"], tbs=(2,))
        stage_2(1)
        attn_tr(0)
        copy_evac(2, fvb, "fvb")
        tok_proj_phase("wv", xhf_hi, xhf_lo, ["xhf3"], tbs=(3,))
        stage_2(2)
        attn_tr(1)
        copy_evac(3, fvb, "fvb")
        ph += 1
        if ph >= upto: break
        ph += 1
        if ph >= upto: break
        load_bias_rows("bo")
        wo_hi = lambda kc, tb: a_hi[tb][:, kc * 128:(kc + 1) * 128]
        wo_lo = lambda kc, tb: a_lo[tb][:, kc * 128:(kc + 1) * 128]
        tok_proj_phase("wo", wo_hi, wo_lo, ["a0"], tbs=(0,))
        stage_2(3)
        attn_tr(2)
        tok_proj_phase("wo", wo_hi, wo_lo, ["a1"], tbs=(1,))
        attn_tr(3)
        resid_evac(0)
        tok_proj_phase("wo", wo_hi, wo_lo, ["a2"], tbs=(2,))
        resid_evac(1)
        resid_evac(2)
        tok_proj_phase("wo", wo_hi, wo_lo, ["a3"], tbs=(3,))
        resid_evac(3)
        ph += 1
        if ph >= upto: break
        ln_stage(tile, slots=(2, 3, 0, 3))
        ph += 1
        if ph >= upto: break
        mlp_phase("fcw", g2_s, gelu_epi)
        ph += 1
        if ph >= upto: break
        hproj_phase("projw", "bpr")
        ph += 1
        if ph >= upto: break
        for tb in range(TB):
            g.op("vector", lambda e, tb=tb: e.tensor_tensor(
                x_t[tb], x_t[tb], x0_t[tb], AL.add),
                reads=[f"xb{tb}"], writes=[f"x{tb}"])
        ln_stage(tile, replace=True, slots=(2, 3, 0, 3))
        ph += 1
        if ph >= upto: break
        mlp_phase("w1w", None, quant_epi)
        ph += 1
        if ph >= upto: break
        hproj_phase("w2w", "bw2")
        ph += 1
        if ph >= upto: break
        ln_stage(tile, final=True)

    for nm, ap, reads in taps:
        dtap = nc.dram_tensor("tap_" + nm, list(ap.shape), ap.dtype,
                              kind="ExternalOutput").ap()
        g.op("gpsimd", lambda e, dtap=dtap, ap=ap: e.dma_start(out=dtap, in_=ap),
             reads=list(reads), inc=16)

    stack = contextlib.ExitStack()
    with stack:
        engines, mk = g.emit(nc, stack)
        with nc.Block() as block:
            for e in engines:
                getattr(block, e)(mk(e))
    return nc


def stage_inputs(inp):
    f32 = lambda a: np.ascontiguousarray(np.asarray(a, dtype=np.float32))
    x = f32(inp["L0_attn_q_x"]).reshape(NCORES * TOKENS, D)
    com = {
        "wq": f32(inp["L0_attn_wq"]),
        "wk": f32(inp["L0_attn_wk"]),
        "wv": f32(inp["L0_attn_wv"]),
        "wo": f32(inp["L0_attn_wo"]),
        "fcw": f32(np.asarray(inp["L0_attn_fc_w"], np.float32)
                   .reshape(KC, 128, MO, 128).transpose(2, 1, 0, 3).reshape(MLP, D)),
        "projw": f32(inp["L0_attn_proj_w"]),
        "w1w": f32(np.asarray(inp["L0_ffn_W1"], np.float32)
                   .reshape(KC, 128, MO, 128).transpose(2, 1, 0, 3).reshape(MLP, D)),
        "w2w": f32(inp["L0_ffn_W2"]),
        "g1": f32(np.asarray(inp["L0_attn_ln1_w"], np.float32).reshape(KC, 128).T),
        "be1": f32(np.asarray(inp["L0_attn_ln1_b"], np.float32).reshape(KC, 128).T),
        "g2": f32(np.asarray(inp["L0_attn_ln2_w"], np.float32).reshape(KC, 128).T),
        "be2": f32(np.asarray(inp["L0_attn_ln2_b"], np.float32).reshape(KC, 128).T),
        "fcb": f32(np.asarray(inp["L0_attn_fc_b"], np.float32).reshape(MO, 128).T),
        "b1f": f32(np.asarray(inp["L0_ffn_b1"], np.float32).reshape(MO, 128).T),
        "th": f32(np.asarray(inp["L0_ffn_quantum_phases"], np.float32).reshape(MO, 128).T),
        "wr": f32(np.asarray(inp["L0_ffn_quantum_weights_real"], np.float32).reshape(MO, 128).T),
        "wi": f32(np.asarray(inp["L0_ffn_quantum_weights_imag"], np.float32).reshape(MO, 128).T),
        "bq": f32(inp["L0_attn_bq"]).reshape(1, D),
        "bk": f32(inp["L0_attn_bk"]).reshape(1, D),
        "bv": f32(inp["L0_attn_bv"]).reshape(1, D),
        "bo": f32(inp["L0_attn_bo"]).reshape(1, D),
        "bpr": f32(inp["L0_attn_proj_b"]).reshape(1, D),
        "bw2": f32(inp["L0_ffn_b2"]).reshape(1, D),
        "eye": np.eye(128, dtype=ml_dtypes.bfloat16),
    }
    in_maps = []
    for c in range(NCORES):
        m = dict(com)
        m["x"] = np.ascontiguousarray(x[c * TOKENS:(c + 1) * TOKENS])
        in_maps.append(m)
    return in_maps


def kernel(**inputs):
    in_maps = stage_inputs(inputs)
    nc = build_nc()
    res = run_bass_kernel_spmd(nc, in_maps, core_ids=list(range(NCORES)))
    y = np.concatenate([np.asarray(res.results[c]["y"]) for c in range(NCORES)],
                       axis=0)
    return np.ascontiguousarray(y.reshape(8, 4096, 1024), dtype=np.float32)


# version 13 (final)
# speedup vs baseline: 1.0898x; 1.0488x over previous
import contextlib
import numpy as np
import ml_dtypes
import concourse.bass as bass
import concourse.mybir as mybir
from concourse.bass_utils import run_bass_kernel_spmd

F32 = mybir.dt.float32
BF16 = mybir.dt.bfloat16
AL = mybir.AluOpType
AF = mybir.ActivationFunctionType
AX = mybir.AxisListType.X

NCORES = 8
TOKENS = 4096
TT = 512
NT = TOKENS // TT
TB = TT // 128
D, KC, MLP, MO, H, HD = 1024, 8, 4096, 32, 16, 64
HALF_PI = float(np.float32(np.pi / 2))
TWO_PI_HI = float(np.float32(2 * np.pi))
TWO_PI_LO = float(np.float32(2 * np.pi - np.float64(np.float32(2 * np.pi))))
PI = float(np.pi)
PI_CLAMP = 3.1415925


SERIALIZE = False


class Prog:

    def __init__(self):
        self.ops = []
        self.writer = {}
        self.readers = {}

    def op(self, engine, fn, reads=(), writes=(), inc=1, drain=False):
        drain = drain or engine in ("vector", "scalar")
        if SERIALIZE == 1:
            reads = list(reads) + ["SERIAL"]
            writes = list(writes) + ["SERIAL"]
        elif SERIALIZE == 2 and engine == "sync":
            reads = list(reads) + ["SERIAL"]
            writes = list(writes) + ["SERIAL"]
        elif SERIALIZE == 3 and engine == "tensor":
            reads = list(reads) + ["SERIAL"]
            writes = list(writes) + ["SERIAL"]
        elif SERIALIZE == 4 and engine in ("vector", "scalar"):
            reads = list(reads) + ["SERIAL"]
            writes = list(writes) + ["SERIAL"]
        i = len(self.ops)
        deps = set()
        for r in reads:
            w = self.writer.get(r)
            if w is not None:
                deps.add(w)
        for r in writes:
            w = self.writer.get(r)
            if w is not None:
                deps.add(w)
            deps.update(self.readers.get(r, ()))
        self.ops.append({"e": engine, "fn": fn, "deps": deps, "inc": inc,
                         "drain": drain})
        for r in reads:
            self.readers.setdefault(r, []).append(i)
        for r in writes:
            self.writer[r] = i
            self.readers[r] = []
        return i

    def emit(self, nc, stack):
        engines = []
        for o in self.ops:
            if o["e"] not in engines:
                engines.append(o["e"])
        ref = set()
        for o in self.ops:
            for d in o["deps"]:
                if self.ops[d]["e"] != o["e"]:
                    ref.add(d)
        for i, o in enumerate(self.ops):
            if o["e"] == "gpsimd" and o["inc"] == 16:
                ref.add(i)
        NPOOL = 8
        pool_engines = {o["e"] for o in self.ops if o["inc"] == 16}
        sems = {}
        pools = {}
        for e in engines:
            sems[e] = stack.enter_context(nc.semaphore("sem_" + e))
            if e in pool_engines:
                pools[e] = [stack.enter_context(nc.semaphore(f"dmap_{e}{k}"))
                            for k in range(NPOOL)]
        cum = {e: 0 for e in engines}
        pcount = {e: [0] * NPOOL for e in pool_engines}
        pidx = {e: 0 for e in pool_engines}
        semkey = [None] * len(self.ops)
        for i, o in enumerate(self.ops):
            e = o["e"]
            if o["inc"] == 16:
                k = pidx[e]
                pidx[e] = (k + 1) % NPOOL
                pcount[e][k] += 16
                semkey[i] = (("pool", e, k), pcount[e][k])
            else:
                if i in ref:
                    cum[e] += o["inc"]
                semkey[i] = (("eng", e), cum[e])

        def sem_of(key):
            return pools[key[1]][key[2]] if key[0] == "pool" else sems[key[1]]

        final_gp = dict(pcount.get("gpsimd", {} and None) and {}) if False else None
        gp_final = (list(enumerate(pcount["gpsimd"]))
                    if "gpsimd" in pcount else [])

        def mk(e):
            def emit_engine(eng):
                wm = {}
                for i, o in enumerate(self.ops):
                    if o["e"] != e:
                        continue
                    for d in sorted(o["deps"]):
                        de = self.ops[d]["e"]
                        if de == e:
                            continue
                        key, v = semkey[d]
                        if wm.get(key, 0) < v:
                            eng.wait_ge(sem_of(key), v)
                            wm[key] = v
                    inst = o["fn"](eng)
                    if o["drain"]:
                        inst = eng.drain()
                    if o["inc"] == 16 or i in ref:
                        key, _ = semkey[i]
                        inst.then_inc(sem_of(key), o["inc"])
                if e == "gpsimd":
                    for k, v in gp_final:
                        if v:
                            eng.wait_ge(pools["gpsimd"][k], v)
            return emit_engine

        return engines, mk


def _bc(ap, dim, count):
    lst = [list(p) for p in ap.ap]
    assert lst[dim][1] == 1
    lst[dim] = [0, count]
    return bass.AP(ap.tensor, ap.offset, lst)


def _swap_free(ap3):
    lst = [list(p) for p in ap3.ap]
    return bass.AP(ap3.tensor, ap3.offset, [lst[0], lst[2], lst[1]])


def build_nc(n_tiles=NT, upto=99, taps=()):
    nc = bass.Bass("TRN2", target_bir_lowering=False)

    d_x = nc.dram_tensor("x", [TOKENS, D], F32, kind="ExternalInput").ap()
    d_w = {}
    for nm in ("wq", "wk", "wv", "wo"):
        d_w[nm] = nc.dram_tensor(nm, [D, D], F32, kind="ExternalInput").ap()
    for nm in ("fcw", "projw", "w1w", "w2w"):
        d_w[nm] = nc.dram_tensor(nm, [MLP, D], F32, kind="ExternalInput").ap()
    d_g1 = nc.dram_tensor("g1", [128, KC], F32, kind="ExternalInput").ap()
    d_be1 = nc.dram_tensor("be1", [128, KC], F32, kind="ExternalInput").ap()
    d_g2 = nc.dram_tensor("g2", [128, KC], F32, kind="ExternalInput").ap()
    d_be2 = nc.dram_tensor("be2", [128, KC], F32, kind="ExternalInput").ap()
    d_fcb = nc.dram_tensor("fcb", [128, MO], F32, kind="ExternalInput").ap()
    d_b1f = nc.dram_tensor("b1f", [128, MO], F32, kind="ExternalInput").ap()
    d_th = nc.dram_tensor("th", [128, MO], F32, kind="ExternalInput").ap()
    d_wrr = nc.dram_tensor("wr", [128, MO], F32, kind="ExternalInput").ap()
    d_wii = nc.dram_tensor("wi", [128, MO], F32, kind="ExternalInput").ap()
    d_brow = {nm: nc.dram_tensor(nm, [1, D], F32, kind="ExternalInput").ap()
              for nm in ("bq", "bk", "bv", "bo", "bpr", "bw2")}
    d_eye = nc.dram_tensor("eye", [128, 128], BF16, kind="ExternalInput").ap()
    d_y = nc.dram_tensor("y", [TOKENS, D], F32, kind="ExternalOutput").ap()
    d_rows = nc.dram_tensor("rows_scr", [12, D], BF16, kind="Internal").ap()
    d_wscr = {}
    for nm in ("wq", "wk", "wv", "wo"):
        d_wscr[nm] = nc.dram_tensor("scr_" + nm, [D, 2 * D], BF16, kind="Internal").ap()
    for nm in ("fcw", "projw", "w1w", "w2w"):
        d_wscr[nm] = nc.dram_tensor("scr_" + nm, [MLP, 2 * D], BF16, kind="Internal").ap()

    sb = lambda nm, sh, dt=F32: nc.alloc_sbuf_tensor(nm, sh, dt).ap()
    x_t = [sb(f"x{t}", [128, D]) for t in range(TB)]
    x0_t = [sb(f"xb{t}", [128, D]) for t in range(TB)]
    XN = [f"x{t}" for t in range(TB)]
    XBN = [f"xb{t}" for t in range(TB)]
    xhf = [sb(f"xhf{i}", [128, KC * TT], BF16) for i in range(2)]
    a_hi = [sb(f"ahi{t}", [128, D], BF16) for t in range(TB)]
    a_lo = [sb(f"alo{t}", [128, D], BF16) for t in range(TB)]
    xh_tm2 = [sb(f"xh_tm{i}", [128, D], BF16) for i in range(2)]
    xl_tm2 = [sb(f"xl_tm{i}", [128, D], BF16) for i in range(2)]
    jnk_tm = sb("jnk_tm", [128, D], BF16)
    fq = sb("fq", [128, TB * D])
    fkb = sb("fkb", [128, D])
    fvb = sb("fvb", [128, D])
    qkvb = sb("qkvb", [128, D])
    Ab = sb("Ab", [128, TB * H * H])
    qks = sb("qks", [128, H])
    rcp = sb("rcp", [128, H])
    scr = sb("scr", [128, D])
    g32 = sb("g32", [128, TT])
    hbuf = sb("hbuf", [128, TT])
    h_hi = sb("h_hi", [128, MO * TT], BF16)
    h_lo = sb("h_lo", [128, MO * TT], BF16)
    wbuf = [sb("wbuf0", [128, D])]
    wbh = [sb(f"wbh{i}", [128, D], BF16) for i in range(2)]
    wbl = [sb(f"wbl{i}", [128, D], BF16) for i in range(2)]
    NWP = 3
    wpair = [sb(f"wpair{i}", [128, 2 * D], BF16) for i in range(NWP)]
    stat = sb("stat", [128, 3 * TB])
    eye_s = sb("eye_s", [128, 128], BF16)
    ones_r = sb("ones_r", [2, 128], BF16)
    g1_s = sb("g1_s", [128, KC]); be1_s = sb("be1_s", [128, KC])
    g2_s = sb("g2_s", [128, KC]); be2_s = sb("be2_s", [128, KC])
    fcb_s = sb("fcb_s", [128, MO]); b1f_s = sb("b1f_s", [128, MO])
    th_s = sb("th_s", [128, MO]); wr_s = sb("wr_s", [128, MO]); wi_s = sb("wi_s", [128, MO])
    biasfc = sb("biasfc", [128, MO])
    b1sin = sb("b1sin", [128, MO]); b1cos = sb("b1cos", [128, MO])
    P1 = sb("P1", [128, MO]); P2 = sb("P2", [128, MO])
    sinb = sb("sinb", [128, MO]); cosb = sb("cosb", [128, MO])
    qtmp = sb("qtmp", [128, MO]); qmask = sb("qmask", [128, MO]); qpb = sb("qpb", [128, MO])
    brow32 = scr[0:1, :]
    brows2 = sb("brows2", [2, D], BF16)
    browh = brows2[0:1, :]
    browl = brows2[1:2, :]

    pA = nc.alloc_psum_tensor("pA", [128, D], F32).ap()
    pB = nc.alloc_psum_tensor("pB", [128, D], F32).ap()
    pC = nc.alloc_psum_tensor("pC", [128, TT], F32).ap()
    pD = nc.alloc_psum_tensor("pD", [128, TT], F32).ap()
    pE = nc.alloc_psum_tensor("pE", [128, TT], F32).ap()
    pF = nc.alloc_psum_tensor("pF", [128, TT], F32).ap()
    pEb = pE.bitcast(BF16)
    pFb = pF.bitcast(BF16)
    G = [pA[:, 0:TT], pA[:, TT:D], pB[:, 0:TT], pB[:, TT:D], pC, pD, pE, pF]
    GR = ["pA", "pA", "pB", "pB", "pC", "pD", "pE", "pF"]
    ALLG = ["pA", "pB", "pC", "pD", "pE", "pF"]

    g = Prog()

    for dst, src in ((g1_s, d_g1), (be1_s, d_be1), (g2_s, d_g2), (be2_s, d_be2),
                     (fcb_s, d_fcb), (b1f_s, d_b1f), (th_s, d_th), (wr_s, d_wrr),
                     (wi_s, d_wii), (eye_s, d_eye)):
        g.op("gpsimd", lambda e, a=dst, b=src: e.dma_start(out=a, in_=b),
             writes=[dst.tensor.name], inc=16)
    g.op("vector", lambda e: e.memset(ones_r, 1.0), writes=["ones_r"], drain=True)

    def wrap_ladder(e, shift):
        e.tensor_scalar(qtmp, th_s, shift, 0.0, AL.add, AL.add)
        e.drain()
        for _ in range(8):
            e.tensor_scalar(qmask, qtmp, PI, 0.0, AL.is_gt, AL.add)
            e.drain()
            e.scalar_tensor_tensor(qtmp, qmask, -TWO_PI_HI, qtmp, AL.mult, AL.add)
            e.drain()
            e.scalar_tensor_tensor(qtmp, qmask, -TWO_PI_LO, qtmp, AL.mult, AL.add)
            e.drain()
            e.tensor_scalar(qmask, qtmp, -PI, 0.0, AL.is_lt, AL.add)
            e.drain()
            e.scalar_tensor_tensor(qtmp, qmask, TWO_PI_HI, qtmp, AL.mult, AL.add)
            e.drain()
            e.scalar_tensor_tensor(qtmp, qmask, TWO_PI_LO, qtmp, AL.mult, AL.add)
            e.drain()
        return e.tensor_scalar(qtmp, qtmp, PI_CLAMP, -PI_CLAMP, AL.min, AL.max)

    g.op("vector", lambda e: wrap_ladder(e, 0.0),
         reads=["th_s"], writes=["qtmp", "qmask"], drain=True)
    g.op("scalar", lambda e: e.activation(sinb, qtmp, AF.Sin),
         reads=["qtmp"], writes=["sinb"], drain=True)
    g.op("vector", lambda e: wrap_ladder(e, HALF_PI),
         reads=["th_s", "sinb"], writes=["qtmp", "qmask"], drain=True)
    g.op("scalar", lambda e: e.activation(cosb, qtmp, AF.Sin),
         reads=["qtmp"], writes=["cosb"], drain=True)

    def qfin(e):
        e.tensor_tensor(qpb, cosb, wr_s, AL.mult)
        e.tensor_tensor(qtmp, sinb, wi_s, AL.mult)
        e.drain()
        e.tensor_tensor(qpb, qpb, qtmp, AL.add)
        e.drain()
        e.tensor_tensor(qpb, qpb, wi_s, AL.mult)
        e.drain()
        e.tensor_scalar(qpb, qpb, 0.1, 0.0, AL.mult, AL.add)
        e.drain()
        e.tensor_tensor(P1, qpb, sinb, AL.mult)
        e.tensor_tensor(P2, qpb, cosb, AL.mult)
        e.tensor_scalar(b1sin, b1f_s, 0.1, 0.0, AL.mult, AL.add)
        return e.tensor_scalar(b1cos, b1f_s, 0.1, HALF_PI, AL.mult, AL.add)
    g.op("vector", qfin,
         reads=["sinb", "cosb", "wr_s", "wi_s", "b1f_s"],
         writes=["P1", "P2", "qpb", "qtmp", "b1sin", "b1cos"], drain=True)

    def prep_weight_slow(wname):
        nrows = d_w[wname].shape[0]
        for c in range(nrows // 128):
            g.op("sync", lambda e, c=c, wname=wname: e.dma_start(
                out=wbuf[0], in_=d_w[wname][c * 128:(c + 1) * 128, :]),
                writes=["wbuf0"], inc=16)
            def cast_plain(e):
                e.tensor_copy(wbh[0], wbuf[0])
                e.drain()
                return e.tensor_tensor(wbl[0], wbuf[0], wbh[0], AL.subtract)
            g.op("vector", cast_plain, reads=["wbuf0"], writes=["wbh0", "wbl0"])
            g.op("sync", lambda e, c=c, wname=wname: e.dma_start(
                out=d_wscr[wname][c * 128:(c + 1) * 128, 0:D], in_=wbh[0]),
                reads=["wbh0"], writes=[f"wscr_{wname}"], inc=16)
            g.op("sync", lambda e, c=c, wname=wname: e.dma_start(
                out=d_wscr[wname][c * 128:(c + 1) * 128, D:2 * D], in_=wbl[0]),
                reads=["wbl0"], writes=[f"wscr_{wname}"], inc=16)

    def prep_weight(wname, gamma=None, beta=None, bias_col_psum=False):
        nrows = d_w[wname].shape[0]
        nch = nrows // 128
        for c in range(nch):
            j = c % 2
            win = wpair[j].bitcast(F32)
            whi = wpair[2][:, 0:D]
            wlo = wpair[2][:, D:2 * D]
            g.op("sync", lambda e, c=c, win=win, wname=wname: e.dma_start(
                out=win, in_=d_w[wname][c * 128:(c + 1) * 128, :]),
                writes=[f"wpair{j}"], inc=16)
            if gamma is None:
                def cast_plain(e, win=win, whi=whi, wlo=wlo):
                    e.tensor_copy(whi, win)
                    e.drain()
                    return e.tensor_tensor(wlo, win, whi, AL.subtract)
                g.op("vector", cast_plain, reads=[f"wpair{j}"],
                     writes=["wpair2"])
            elif gamma == "bcast":
                def cast_fold2(e, win=win, whi=whi, wlo=wlo):
                    gb = _bc(g2_s.rearrange("p (a b) -> p a b", b=1), 2, 128)
                    w3 = win.rearrange("p (a b) -> p a b", a=KC)
                    e.tensor_tensor(scr.rearrange("p (a b) -> p a b", a=KC),
                                    w3, gb, AL.mult)
                    e.drain()
                    e.tensor_copy(whi, scr)
                    e.drain()
                    return e.tensor_tensor(wlo, scr, whi, AL.subtract)
                g.op("vector", cast_fold2,
                     reads=[f"wpair{j}", "g2_s"], writes=["wpair2", "scr"])
            else:
                def cast_gam(e, c=c, win=win, whi=whi, wlo=wlo, gamma=gamma):
                    e.tensor_scalar(whi, win, gamma[:, c % KC:c % KC + 1], 0.0,
                                    AL.mult, AL.add)
                    e.drain()
                    return e.scalar_tensor_tensor(
                        wlo, win, gamma[:, c % KC:c % KC + 1], whi,
                        AL.mult, AL.subtract)
                g.op("vector", cast_gam,
                     reads=[f"wpair{j}", gamma.tensor.name], writes=["wpair2"])
            g.op("sync", lambda e, c=c, wname=wname: e.dma_start(
                out=d_wscr[wname][c * 128:(c + 1) * 128, :], in_=wpair[2]),
                reads=["wpair2"], writes=[f"wscr_{wname}"], inc=16)
            if beta is not None and not bias_col_psum:
                g.op("tensor", lambda e, c=c, win=win: (
                    e.matmul(pC[0:1, :], beta[:, c:c + 1], win[:, 0:TT],
                             start=(c == 0), stop=(c == KC - 1)),
                    e.matmul(pD[0:1, :], beta[:, c:c + 1], win[:, TT:D],
                             start=(c == 0), stop=(c == KC - 1)))[-1],
                    reads=[f"wpair{j}", beta.tensor.name], writes=["pC", "pD"])
            if beta is not None and bias_col_psum:
                g.op("tensor", lambda e, c=c, win=win: [
                    e.matmul(pC[:, c:c + 1], win[:, kc * 128:(kc + 1) * 128],
                             beta[:, kc:kc + 1], start=(kc == 0),
                             stop=(kc == KC - 1))
                    for kc in range(KC)][-1],
                    reads=[f"wpair{j}", beta.tensor.name], writes=["pC"])

    ROWIDX = {"bq": 0, "bk": 1, "bv": 2, "bo": 3, "bpr": 4, "bw2": 5}

    def prep_bias_row(nm, with_corr=False):
        r = ROWIDX[nm]
        g.op("gpsimd", lambda e: e.dma_start(out=brow32, in_=d_brow[nm]),
             writes=["brow32", "scr"], inc=16)
        if with_corr:
            g.op("vector", lambda e: (
                e.tensor_tensor(brow32[:, 0:TT], pC[0:1, :], brow32[:, 0:TT], AL.add),
                e.tensor_tensor(brow32[:, TT:D], pD[0:1, :], brow32[:, TT:D], AL.add))[-1],
                reads=["pC", "pD"], writes=["brow32", "scr", "pC", "pD"], drain=True)
        hi_st = qkvb[0:1, 0:TT].bitcast(BF16)
        lo_st = qkvb[0:1, TT:D].bitcast(BF16)
        def row_fin(e):
            e.tensor_copy(hi_st, brow32)
            e.drain()
            return e.tensor_tensor(lo_st, brow32, hi_st, AL.subtract)
        g.op("vector", row_fin, reads=["brow32", "scr"],
             writes=["rowstage", "qkvb"], drain=True)
        g.op("gpsimd", lambda e, r=r: e.dma_start(out=d_rows[r:r + 1, :], in_=hi_st),
             reads=["rowstage"], writes=[f"rows{r}"], inc=16)
        g.op("gpsimd", lambda e, r=r: e.dma_start(out=d_rows[6 + r:7 + r, :], in_=lo_st),
             reads=["rowstage"], writes=[f"rows{6 + r}"], inc=16)

    prep_weight("wq", gamma=g1_s, beta=be1_s)
    prep_bias_row("bq", with_corr=True)
    prep_weight("wk", gamma=g1_s, beta=be1_s)
    prep_bias_row("bk", with_corr=True)
    prep_weight("wv", gamma=g1_s, beta=be1_s)
    prep_bias_row("bv", with_corr=True)
    prep_weight("fcw", gamma="bcast", beta=be2_s, bias_col_psum=True)
    g.op("vector", lambda e: e.tensor_tensor(biasfc, pC[:, 0:MO], fcb_s, AL.add),
         reads=["pC", "fcb_s"], writes=["biasfc", "pC"], drain=True)
    prep_bias_row("bo")
    prep_bias_row("bpr")
    prep_bias_row("bw2")
    prep_weight("wo")
    prep_weight("projw")
    prep_weight("w1w")
    prep_weight("w2w")

    def load_bias_rows(nm):
        r = ROWIDX[nm]
        g.op("sync", lambda e, r=r: e.dma_start(out=browh, in_=d_rows[r:r + 1, :]),
             reads=[f"rows{r}"], writes=["browh"], inc=16)
        g.op("sync", lambda e, r=r: e.dma_start(out=browl, in_=d_rows[6 + r:7 + r, :]),
             reads=[f"rows{6 + r}"], writes=["browl"], inc=16)

    def tok_proj_phase(wname, lhs_hi_fn, lhs_lo_fn, lhs_reads, tbs=(0, 1, 2, 3)):
        gis = [tb * 2 + nb for tb in tbs for nb in range(2)]
        wr = sorted({GR[gi] for gi in gis})
        for kc in range(KC):
            j = (kc + (0 if tbs[0] == 0 else KC)) % NWP if False else kc % NWP
            g.op("sync", lambda e, kc=kc, j=j: e.dma_start(
                out=wpair[j], in_=d_wscr[wname][kc * 128:(kc + 1) * 128, :]),
                reads=[f"wscr_{wname}"], writes=[f"wpair{j}"], inc=16)

            def mm(e, kc=kc, j=j):
                last = None
                if kc == 0:
                    for gi in gis:
                        nb = gi % 2
                        e.matmul(G[gi], ones_r, brows2[:, nb * TT:(nb + 1) * TT],
                                 start=True, stop=False)
                for gi in gis:
                    tb, nb = gi // 2, gi % 2
                    lh = lhs_hi_fn(kc, tb)
                    ll = lhs_lo_fn(kc, tb)
                    rh = wpair[j][:, nb * TT:(nb + 1) * TT]
                    rl = wpair[j][:, D + nb * TT:D + (nb + 1) * TT]
                    e.matmul(G[gi], lh, rh, start=False, stop=False)
                    e.matmul(G[gi], lh, rl, start=False, stop=False)
                    last = e.matmul(G[gi], ll, rh, start=False, stop=(kc == KC - 1))
                return last
            rds = [f"wpair{j}", "ones_r"] + lhs_reads
            if kc == 0:
                rds += ["browh", "browl"]
            g.op("tensor", mm, reads=rds, writes=wr)

    def xhf_hi(kc, tb):
        return xhf[0][:, kc * TT + tb * 128: kc * TT + (tb + 1) * 128]

    def xhf_lo(kc, tb):
        return xhf[1][:, kc * TT + tb * 128: kc * TT + (tb + 1) * 128]

    def elu_evac(tb, dst, dst_res):
        g0, g1 = G[tb * 2], G[tb * 2 + 1]
        r0, r1 = GR[tb * 2], GR[tb * 2 + 1]
        g.op("vector", lambda e: (
            e.tensor_scalar(scr[:, 0:TT], g0, 0.0, None, AL.min),
            e.tensor_scalar(scr[:, TT:D], g1, 0.0, None, AL.min),
            e.tensor_scalar(dst[:, 0:TT], g0, 0.0, None, AL.max),
            e.tensor_scalar(dst[:, TT:D], g1, 0.0, None, AL.max))[-1],
            reads=[r0, r1], writes=["scr", dst_res, r0, r1])
        g.op("scalar", lambda e: e.activation(scr, scr, AF.Exp),
             reads=["scr"], writes=["scr"])
        g.op("vector", lambda e: e.tensor_tensor(dst, dst, scr, AL.add),
             reads=["scr"], writes=[dst_res])

    def copy_evac(tb, dst, dst_res):
        g0, g1 = G[tb * 2], G[tb * 2 + 1]
        r0, r1 = GR[tb * 2], GR[tb * 2 + 1]
        def ev(e, g0=g0, g1=g1):
            dv = dst.rearrange("p (a b) -> p a b", a=HD)
            s0 = bass.AP(g0.tensor, g0.offset, [list(g0.ap[0]), [1, HD], [HD, 8]])
            s1 = bass.AP(g1.tensor, g1.offset, [list(g1.ap[0]), [1, HD], [HD, 8]])
            e.tensor_copy(dv[:, :, 0:8], s0)
            return e.tensor_copy(dv[:, :, 8:16], s1)
        g.op("vector", ev, reads=[r0, r1], writes=[dst_res, r0, r1])

    def resid_evac(tb, xs=None, xsn=None):
        xs = xs or x_t
        xsn = xsn or XN
        g0, g1 = G[tb * 2], G[tb * 2 + 1]
        r0, r1 = GR[tb * 2], GR[tb * 2 + 1]
        g.op("vector", lambda e, tb=tb, xt=xs[tb]: (
            e.tensor_tensor(xt[:, 0:TT], g0, xt[:, 0:TT], AL.add),
            e.tensor_tensor(xt[:, TT:D], g1, xt[:, TT:D], AL.add))[-1],
            reads=[r0, r1], writes=[xsn[tb], r0, r1])

    def stage_A(tb):
        fq_t = fq[:, tb * D:(tb + 1) * D]
        def fn(e, tb=tb):
            fk3 = fkb.rearrange("p (g n) -> p g n", g=H)
            pr3 = scr.rearrange("p (g n) -> p g n", g=H)
            last = None
            for h in range(H):
                fqh = fq_t[:, h * HD:(h + 1) * HD].rearrange("p (g n) -> p g n", g=1)
                e.tensor_tensor(pr3, fk3, _bc(fqh, 1, H), AL.mult)
                last = e.tensor_reduce(
                    Ab[:, tb * H * H + h * H: tb * H * H + (h + 1) * H],
                    pr3, AX, AL.add, opt_output=False)
            return last
        g.op("vector", fn, reads=[f"fq{tb}", "fkb"], writes=[f"A{tb}", "scr"],
             drain=True)

    def stage_2(tb):
        A_t = Ab[:, tb * H * H:(tb + 1) * H * H]
        def fn(e, tb=tb, A_t=A_t):
            e.tensor_reduce(qks, A_t.rearrange("p (g n) -> p g n", g=H), AX,
                            AL.add, opt_output=False)
            e.drain()
            e.tensor_scalar(rcp, qks, 1e-6, 0.0, AL.add, AL.add)
            e.drain()
            e.reciprocal(rcp, rcp)
            e.drain()
            fve = fvb.rearrange("p (a b) -> p a b", a=HD)
            pre = scr.rearrange("p (a b) -> p a b", a=HD)
            for h in range(H):
                Ah = A_t[:, h * H:(h + 1) * H]
                Ab_ = bass.AP(Ah.tensor, Ah.offset, [list(Ah.ap[0]), [0, HD],
                                                     list(Ah.ap[1])])
                e.tensor_tensor(pre, fve, Ab_, AL.mult)
                e.tensor_reduce(qkvb[:, h * HD:(h + 1) * HD],
                                pre, AX, AL.add, opt_output=False)
            e.drain()
            for h in range(H):
                sl = slice(h * HD, (h + 1) * HD)
                e.tensor_scalar(a_hi[tb][:, sl], qkvb[:, sl], rcp[:, h:h + 1],
                                0.0, AL.mult, AL.add)
            last = None
            for h in range(H):
                sl = slice(h * HD, (h + 1) * HD)
                last = e.scalar_tensor_tensor(a_lo[tb][:, sl], qkvb[:, sl],
                                              rcp[:, h:h + 1], a_hi[tb][:, sl],
                                              AL.mult, AL.subtract)
            return last
        g.op("vector", fn, reads=[f"A{tb}", "fvb"],
             writes=[f"a{tb}", "scr", "qkvb", "qks", "rcp"], drain=True)

    pAb = pA.bitcast(BF16)
    pBb = pB.bitcast(BF16)
    pCb = pC.bitcast(BF16)
    pDb = pD.bitcast(BF16)
    TRP = [((pEb, "pE"), (pFb, "pF")),
           ((pCb, "pC"), (pDb, "pD")),
           ((pAb[:, 0:D], "pA"), (pAb[:, D:2 * D], "pA")),
           ((pBb[:, 0:D], "pB"), (pBb[:, D:2 * D], "pB"))]

    def transpose_pair(src_hi, src_lo, reads, dst_view_fn, wr_res, slot=0):
        (ph, phr), (pl, plr) = TRP[slot]
        g.op("tensor", lambda e: ([
            e.transpose(ph[:, b * 128:(b + 1) * 128],
                        src_hi[:, b * 128:(b + 1) * 128], eye_s)
            for b in range(KC)] + [
            e.transpose(pl[:, b * 128:(b + 1) * 128],
                        src_lo[:, b * 128:(b + 1) * 128], eye_s)
            for b in range(KC)])[-1],
            reads=list(reads) + ["eye_s"], writes=[phr, plr])
        def evac(e):
            d0, d1 = dst_view_fn()
            e.tensor_copy(d0, ph.rearrange("p (a b) -> p a b", a=KC))
            return e.tensor_copy(d1, pl.rearrange("p (a b) -> p a b", a=KC))
        g.op("vector", evac, reads=[phr, plr], writes=list(wr_res) + [phr, plr])

    def ln_stage(tile, final=False, replace=False, slots=(2, 3, 0, 1),
                 xs=None, xsn=None):
        xs = xs or x_t
        xsn = xsn or XN
        for tb in range(TB):
            cn, cs = 3 * tb, 3 * tb + 2
            def stats_v(e, tb=tb, cn=cn, cs=cs, xt=xs[tb]):
                e.reduce_sum(stat[:, cs:cs + 1], xt, AX, opt_output=False)
                e.drain()
                return e.tensor_scalar(stat[:, cn:cn + 1], stat[:, cs:cs + 1],
                                       -1.0 / D, 0.0, AL.mult, AL.add)
            g.op("vector", stats_v, reads=[xsn[tb]], writes=[f"st{tb}"],
                 drain=True)
        for tb in range(TB):
            cn, cs = 3 * tb, 3 * tb + 2
            def stats_s(e, tb=tb, cn=cn, cs=cs, xt=xs[tb]):
                return e.activation(jnk_tm, xt, AF.Square,
                                    bias=stat[:, cn:cn + 1],
                                    accum_out=stat[:, cs:cs + 1])
            g.op("scalar", stats_s, reads=[xsn[tb], f"st{tb}"],
                 writes=[f"st{tb}", "jnk"], drain=True)
        for tb in range(TB):
            cr, cs = 3 * tb + 1, 3 * tb + 2
            g.op("vector", lambda e, cr=cr, cs=cs: e.tensor_scalar(
                stat[:, cr:cr + 1], stat[:, cs:cs + 1], 1.0 / D, 1e-5,
                AL.mult, AL.add),
                reads=[f"st{tb}"], writes=[f"st{tb}"], drain=True)
        for tb in range(TB):
            cr, cs = 3 * tb + 1, 3 * tb + 2
            def rstd_s(e, cr=cr, cs=cs):
                e.activation(stat[:, cs:cs + 1], stat[:, cr:cr + 1], AF.Ln)
                e.drain()
                return e.activation(stat[:, cr:cr + 1], stat[:, cs:cs + 1],
                                    AF.Exp, scale=-0.5)
            g.op("scalar", rstd_s, reads=[f"st{tb}"], writes=[f"st{tb}"],
                 drain=True)
        if final:
            for tb in range(TB):
                cn, cr = 3 * tb, 3 * tb + 1
                g.op("vector", lambda e, tb=tb, cn=cn, cr=cr, xt=xs[tb]:
                    e.tensor_scalar(
                        xt, xt, stat[:, cn:cn + 1], stat[:, cr:cr + 1],
                        AL.add, AL.mult),
                    reads=[f"st{tb}"], writes=[xsn[tb]])
                r0 = tile * TT + tb * 128
                g.op("gpsimd", lambda e, tb=tb, r0=r0, xt=xs[tb]: e.dma_start(
                    out=d_y[r0:r0 + 128, :], in_=xt),
                    reads=[xsn[tb]], inc=16)
            return
        for tb in range(TB):
            cn, cr = 3 * tb, 3 * tb + 1
            def apply_v(e, tb=tb, cn=cn, cr=cr, replace=replace, xt=xs[tb]):
                e.tensor_scalar(scr, xt, stat[:, cn:cn + 1],
                                stat[:, cr:cr + 1], AL.add, AL.mult)
                e.tensor_copy(xh_tm2[tb % 2], scr)
                ret = e.tensor_tensor(xl_tm2[tb % 2], scr, xh_tm2[tb % 2],
                                      AL.subtract)
                if replace:
                    ret = e.tensor_copy(xt, scr)
                return ret
            g.op("vector", apply_v, reads=[f"st{tb}"],
                 writes=[f"xt2_{tb % 2}", "scr", xsn[tb]])
            def dv(tb=tb):
                v0 = xhf[0].rearrange("p (a b) -> p a b", a=KC)[:, :, tb * 128:(tb + 1) * 128]
                v1 = xhf[1].rearrange("p (a b) -> p a b", a=KC)[:, :, tb * 128:(tb + 1) * 128]
                return v0, v1
            transpose_pair(xh_tm2[tb % 2], xl_tm2[tb % 2], [f"xt2_{tb % 2}"],
                           dv, [f"xhf{tb}"], slot=slots[tb])

    def mlp_phase(wname, gamma, epi):
        for mo in range(MO):
            j = mo % NWP
            g.op("sync", lambda e, mo=mo, j=j: e.dma_start(
                out=wpair[j], in_=d_wscr[wname][mo * 128:(mo + 1) * 128, :]),
                reads=[f"wscr_{wname}"], writes=[f"wpair{j}"], inc=16)
            pf, pf_res = (pC, "pC") if mo % 2 == 0 else (pD, "pD")

            def mm(e, j=j, pf=pf):
                last = None
                for kc in range(KC):
                    lh = wpair[j][:, kc * 128:(kc + 1) * 128]
                    ll = wpair[j][:, D + kc * 128:D + (kc + 1) * 128]
                    rh = xhf[0][:, kc * TT:(kc + 1) * TT]
                    rl = xhf[1][:, kc * TT:(kc + 1) * TT]
                    e.matmul(pf, lh, rh, start=(kc == 0), stop=False)
                    e.matmul(pf, lh, rl, start=False, stop=False)
                    last = e.matmul(pf, ll, rh, start=False, stop=(kc == KC - 1))
                return last
            g.op("tensor", mm,
                 reads=[f"wpair{j}"] + [f"xhf{t}" for t in range(TB)],
                 writes=[pf_res])
            epi(mo, pf, pf_res)

    def gelu_epi(mo, pf, pf_res):
        hsl = slice(mo * TT, (mo + 1) * TT)
        g.op("scalar", lambda e, pf=pf, mo=mo: e.activation(
            g32, pf, AF.Gelu, bias=biasfc[:, mo:mo + 1]),
            reads=[pf_res, "biasfc"], writes=["g32", pf_res])
        g.op("vector", lambda e, hsl=hsl: (
            e.tensor_copy(h_hi[:, hsl], g32),
            e.tensor_tensor(h_lo[:, hsl], g32, h_hi[:, hsl], AL.subtract))[-1],
            reads=["g32"], writes=[f"h{mo}"])

    def quant_epi(mo, pf, pf_res):
        hsl = slice(mo * TT, (mo + 1) * TT)
        g.op("scalar", lambda e, pf=pf, mo=mo: (
            e.activation(scr[:, 0:TT], pf, AF.Sin, scale=0.1,
                         bias=b1sin[:, mo:mo + 1]),
            e.activation(scr[:, TT:D], pf, AF.Sin, scale=0.1,
                         bias=b1cos[:, mo:mo + 1]))[-1],
            reads=[pf_res, "b1sin", "b1cos"], writes=["scr"])
        def qv(e, pf=pf, mo=mo, hsl=hsl):
            e.tensor_scalar(hbuf, pf, b1f_s[:, mo:mo + 1], None, AL.add)
            e.scalar_tensor_tensor(g32, scr[:, TT:D], P1[:, mo:mo + 1], hbuf,
                                   AL.mult, AL.add)
            e.scalar_tensor_tensor(g32, scr[:, 0:TT], P2[:, mo:mo + 1], g32,
                                   AL.mult, AL.add)
            e.tensor_scalar(h_hi[:, hsl], g32, 0.0, None, AL.max)
            return e.scalar_tensor_tensor(h_lo[:, hsl], g32, 0.0, h_hi[:, hsl],
                                          AL.max, AL.subtract)
        g.op("vector", qv,
             reads=[pf_res, "scr", "b1f_s", "P1", "P2"],
             writes=[f"h{mo}", "hbuf", "g32", pf_res])

    def hproj_phase(wname, nm):
        load_bias_rows(nm)
        for tbs in ((0, 1), (2, 3)):
            gis = [tb * 2 + nb for tb in tbs for nb in range(2)]
            wr = sorted({GR[gi] for gi in gis})

            def bias_mm(e, gis=gis):
                last = None
                for gi in gis:
                    nb = gi % 2
                    last = e.matmul(G[gi], ones_r,
                                    brows2[:, nb * TT:(nb + 1) * TT],
                                    start=True, stop=False)
                return last
            g.op("tensor", bias_mm, reads=["browh", "browl", "ones_r"], writes=wr)
            for mo in range(MO):
                j = mo % NWP
                g.op("sync", lambda e, mo=mo, j=j: e.dma_start(
                    out=wpair[j], in_=d_wscr[wname][mo * 128:(mo + 1) * 128, :]),
                    reads=[f"wscr_{wname}"], writes=[f"wpair{j}"], inc=16)

                def mm(e, mo=mo, j=j, gis=gis):
                    last = None
                    for gi in gis:
                        tb, nb = gi // 2, gi % 2
                        lh = h_hi[:, mo * TT + tb * 128: mo * TT + (tb + 1) * 128]
                        ll = h_lo[:, mo * TT + tb * 128: mo * TT + (tb + 1) * 128]
                        rh = wpair[j][:, nb * TT:(nb + 1) * TT]
                        rl = wpair[j][:, D + nb * TT:D + (nb + 1) * TT]
                        e.matmul(G[gi], lh, rh, start=False, stop=False)
                        e.matmul(G[gi], lh, rl, start=False, stop=False)
                        last = e.matmul(G[gi], ll, rh, start=False,
                                        stop=(mo == MO - 1))
                    return last
                g.op("tensor", mm, reads=[f"wpair{j}", f"h{mo}"], writes=wr)
            for tb in tbs:
                resid_evac(tb, S, SN)

    for tile in range(n_tiles):
        S, SN = (x_t, XN) if tile % 2 == 0 else (x0_t, XBN)
        Bk, BkN = (x0_t, XBN) if tile % 2 == 0 else (x_t, XN)
        ph = 0
        if tile == 0:
            for tb in range(TB):
                r0 = tb * 128
                g.op("gpsimd", lambda e, r0=r0, dst=S[tb]: e.dma_start(
                    out=dst, in_=d_x[r0:r0 + 128, :]),
                    writes=[SN[tb]], inc=16)
                g.op("gpsimd", lambda e, r0=r0, dst=Bk[tb]: e.dma_start(
                    out=dst, in_=d_x[r0:r0 + 128, :]),
                    writes=[BkN[tb]], inc=16)
        ph += 1
        if ph >= upto: break
        ln_stage(tile, xs=S, xsn=SN)
        ph += 1
        if ph >= upto: break
        load_bias_rows("bq")
        for tbs in ((0, 1), (2, 3)):
            tok_proj_phase("wq", xhf_hi, xhf_lo,
                           [f"xhf{t}" for t in tbs], tbs=tbs)
            for tb in tbs:
                elu_evac(tb, fq[:, tb * D:(tb + 1) * D], f"fq{tb}")
        ph += 1
        if ph >= upto: break
        load_bias_rows("bk")
        for tbs in ((0, 1), (2, 3)):
            tok_proj_phase("wk", xhf_hi, xhf_lo,
                           [f"xhf{t}" for t in tbs], tbs=tbs)
            for tb in tbs:
                elu_evac(tb, fkb, "fkb")
                stage_A(tb)
        ph += 1
        if ph >= upto: break
        load_bias_rows("bv")
        def attn_tr(tb):
            transpose_pair(a_hi[tb], a_lo[tb], [f"a{tb}"],
                           lambda tb=tb: (
                               a_hi[tb].rearrange("p (a b) -> p a b", a=KC),
                               a_lo[tb].rearrange("p (a b) -> p a b", a=KC)),
                           [f"a{tb}"], slot=(2, 3, 1, 0)[tb])
        tok_proj_phase("wv", xhf_hi, xhf_lo, ["xhf0"], tbs=(0,))
        copy_evac(0, fvb, "fvb")
        tok_proj_phase("wv", xhf_hi, xhf_lo, ["xhf1"], tbs=(1,))
        stage_2(0)
        copy_evac(1, fvb, "fvb")
        tok_proj_phase("wv", xhf_hi, xhf_lo, ["xhf2"], tbs=(2,))
        stage_2(1)
        attn_tr(0)
        copy_evac(2, fvb, "fvb")
        tok_proj_phase("wv", xhf_hi, xhf_lo, ["xhf3"], tbs=(3,))
        stage_2(2)
        attn_tr(1)
        copy_evac(3, fvb, "fvb")
        ph += 1
        if ph >= upto: break
        ph += 1
        if ph >= upto: break
        load_bias_rows("bo")
        wo_hi = lambda kc, tb: a_hi[tb][:, kc * 128:(kc + 1) * 128]
        wo_lo = lambda kc, tb: a_lo[tb][:, kc * 128:(kc + 1) * 128]
        tok_proj_phase("wo", wo_hi, wo_lo, ["a0"], tbs=(0,))
        stage_2(3)
        attn_tr(2)
        tok_proj_phase("wo", wo_hi, wo_lo, ["a1"], tbs=(1,))
        attn_tr(3)
        resid_evac(0, S, SN)
        tok_proj_phase("wo", wo_hi, wo_lo, ["a2"], tbs=(2,))
        resid_evac(1, S, SN)
        resid_evac(2, S, SN)
        tok_proj_phase("wo", wo_hi, wo_lo, ["a3"], tbs=(3,))
        resid_evac(3, S, SN)
        ph += 1
        if ph >= upto: break
        ln_stage(tile, slots=(2, 3, 0, 3), xs=S, xsn=SN)
        ph += 1
        if ph >= upto: break
        mlp_phase("fcw", g2_s, gelu_epi)
        ph += 1
        if ph >= upto: break
        hproj_phase("projw", "bpr")
        ph += 1
        if ph >= upto: break
        for tb in range(TB):
            g.op("vector", lambda e, st=S[tb], bt=Bk[tb]: e.tensor_tensor(
                st, st, bt, AL.add),
                reads=[BkN[tb]], writes=[SN[tb]])
        if tile + 1 < n_tiles:
            for tb in range(TB):
                r0 = (tile + 1) * TT + tb * 128
                g.op("gpsimd", lambda e, r0=r0, dst=Bk[tb]: e.dma_start(
                    out=dst, in_=d_x[r0:r0 + 128, :]),
                    writes=[BkN[tb]], inc=16)
        ln_stage(tile, replace=True, slots=(2, 3, 0, 3), xs=S, xsn=SN)
        ph += 1
        if ph >= upto: break
        mlp_phase("w1w", None, quant_epi)
        ph += 1
        if ph >= upto: break
        hproj_phase("w2w", "bw2")
        ph += 1
        if ph >= upto: break
        ln_stage(tile, final=True, xs=S, xsn=SN)
        if tile + 1 < n_tiles:
            for tb in range(TB):
                r0 = (tile + 1) * TT + tb * 128
                g.op("gpsimd", lambda e, r0=r0, dst=S[tb]: e.dma_start(
                    out=dst, in_=d_x[r0:r0 + 128, :]),
                    writes=[SN[tb]], inc=16)

    for nm, ap, reads in taps:
        dtap = nc.dram_tensor("tap_" + nm, list(ap.shape), ap.dtype,
                              kind="ExternalOutput").ap()
        g.op("gpsimd", lambda e, dtap=dtap, ap=ap: e.dma_start(out=dtap, in_=ap),
             reads=list(reads), inc=16)

    stack = contextlib.ExitStack()
    with stack:
        engines, mk = g.emit(nc, stack)
        with nc.Block() as block:
            for e in engines:
                getattr(block, e)(mk(e))
    return nc


def stage_inputs(inp):
    f32 = lambda a: np.ascontiguousarray(np.asarray(a, dtype=np.float32))
    x = f32(inp["L0_attn_q_x"]).reshape(NCORES * TOKENS, D)
    com = {
        "wq": f32(inp["L0_attn_wq"]),
        "wk": f32(inp["L0_attn_wk"]),
        "wv": f32(inp["L0_attn_wv"]),
        "wo": f32(inp["L0_attn_wo"]),
        "fcw": f32(np.asarray(inp["L0_attn_fc_w"], np.float32)
                   .reshape(KC, 128, MO, 128).transpose(2, 1, 0, 3).reshape(MLP, D)),
        "projw": f32(inp["L0_attn_proj_w"]),
        "w1w": f32(np.asarray(inp["L0_ffn_W1"], np.float32)
                   .reshape(KC, 128, MO, 128).transpose(2, 1, 0, 3).reshape(MLP, D)),
        "w2w": f32(inp["L0_ffn_W2"]),
        "g1": f32(np.asarray(inp["L0_attn_ln1_w"], np.float32).reshape(KC, 128).T),
        "be1": f32(np.asarray(inp["L0_attn_ln1_b"], np.float32).reshape(KC, 128).T),
        "g2": f32(np.asarray(inp["L0_attn_ln2_w"], np.float32).reshape(KC, 128).T),
        "be2": f32(np.asarray(inp["L0_attn_ln2_b"], np.float32).reshape(KC, 128).T),
        "fcb": f32(np.asarray(inp["L0_attn_fc_b"], np.float32).reshape(MO, 128).T),
        "b1f": f32(np.asarray(inp["L0_ffn_b1"], np.float32).reshape(MO, 128).T),
        "th": f32(np.asarray(inp["L0_ffn_quantum_phases"], np.float32).reshape(MO, 128).T),
        "wr": f32(np.asarray(inp["L0_ffn_quantum_weights_real"], np.float32).reshape(MO, 128).T),
        "wi": f32(np.asarray(inp["L0_ffn_quantum_weights_imag"], np.float32).reshape(MO, 128).T),
        "bq": f32(inp["L0_attn_bq"]).reshape(1, D),
        "bk": f32(inp["L0_attn_bk"]).reshape(1, D),
        "bv": f32(inp["L0_attn_bv"]).reshape(1, D),
        "bo": f32(inp["L0_attn_bo"]).reshape(1, D),
        "bpr": f32(inp["L0_attn_proj_b"]).reshape(1, D),
        "bw2": f32(inp["L0_ffn_b2"]).reshape(1, D),
        "eye": np.eye(128, dtype=ml_dtypes.bfloat16),
    }
    in_maps = []
    for c in range(NCORES):
        m = dict(com)
        m["x"] = np.ascontiguousarray(x[c * TOKENS:(c + 1) * TOKENS])
        in_maps.append(m)
    return in_maps


def kernel(**inputs):
    in_maps = stage_inputs(inputs)
    nc = build_nc()
    res = run_bass_kernel_spmd(nc, in_maps, core_ids=list(range(NCORES)))
    y = np.concatenate([np.asarray(res.results[c]["y"]) for c in range(NCORES)],
                       axis=0)
    return np.ascontiguousarray(y.reshape(8, 4096, 1024), dtype=np.float32)
